# version 71
# baseline (speedup 1.0000x reference)
import math
import numpy as np
from contextlib import ExitStack

import concourse.bass as bass
import concourse.mybir as mybir
from concourse.bass_utils import run_bass_kernel_spmd

F32 = mybir.dt.float32
BF16 = mybir.dt.bfloat16
AF = mybir.ActivationFunctionType
OP = mybir.AluOpType

NCORES = 8
D = 1024
T = 2048
INTER = 4096

C1 = 512
NC1 = T // C1
C2 = 256
NC2 = T // C2
TH = 1024
NTH = T // TH
NG = 8
GI = INTER // NG

ENGINES = ("pe", "act", "dve", "pool", "sp")


class Plan:

    def __init__(self):
        self.ops = {e: [] for e in ENGINES}
        self.dma_slots = {"pool": [f"dp{i}" for i in range(8)],
                          "sp": [f"ds{i}" for i in range(12)]}
        self.dma_rr = {"pool": 0, "sp": 0}
        self.sems = (["c_pe", "c_act", "c_dve", "c_pool"]
                     + self.dma_slots["pool"] + self.dma_slots["sp"])
        self.cnt = {s: 0 for s in self.sems}
        self.water = {}
        self.reg = {}

    def _want(self, eng, waits, sem, val):
        if val <= 0:
            return
        if eng == "pe" and sem == "c_pe":
            return
        k = (eng, sem)
        if self.water.get(k, 0) >= val:
            return
        self.water[k] = val
        waits.append((sem, val))

    def emit(self, eng, fn, reads=(), writes=(), dma=False):
        waits = []
        if dma:
            slots = self.dma_slots[eng]
            sem = slots[self.dma_rr[eng] % len(slots)]
            self.dma_rr[eng] += 1
            self._want(eng, waits, sem, self.cnt[sem])
        else:
            sem = "c_" + eng
        for r in reads:
            info = self.reg.get(r)
            if info and info["w"] is not None:
                self._want(eng, waits, *info["w"])
        for w in writes:
            info = self.reg.get(w)
            if info:
                if info["w"] is not None:
                    self._want(eng, waits, *info["w"])
                for rd in info["rs"]:
                    self._want(eng, waits, *rd)
        self.cnt[sem] += 16 if dma else 1
        val = self.cnt[sem]
        self.ops[eng].append((waits, fn, sem))
        for r in reads:
            self.reg.setdefault(r, {"w": None, "rs": []})["rs"].append((sem, val))
        for w in writes:
            self.reg[w] = {"w": (sem, val), "rs": []}

    def barrier(self, compute_only=False, skip=()):
        snap = dict(self.cnt)
        for e in ENGINES:
            if e in skip:
                continue
            waits = []
            for s, v in snap.items():
                if s == "c_" + e:
                    continue
                if compute_only and not s.startswith("c_"):
                    continue
                self._want(e, waits, s, v)
            if waits:
                self.ops[e].append((waits, None, None))

    def final_wait(self, eng, queue):
        waits = [(sem, self.cnt[sem]) for sem in self.dma_slots[queue] if self.cnt[sem] > 0]
        self.ops[eng].append((waits, None, None))


def build_nc():
    nc = bass.Bass("TRN2", target_bir_lowering=False)

    xT = nc.dram_tensor("xT", [D, T], F32, kind="ExternalInput").ap()
    wqkvT = nc.dram_tensor("wqkvT", [D, 3 * D], F32, kind="ExternalInput").ap()
    woutT = nc.dram_tensor("woutT", [D, D], F32, kind="ExternalInput").ap()
    w1 = nc.dram_tensor("w1", [D, INTER], F32, kind="ExternalInput").ap()
    w2 = nc.dram_tensor("w2", [INTER, D], F32, kind="ExternalInput").ap()
    rf2 = nc.dram_tensor("rf2", [128, 128], F32, kind="ExternalInput").ap()
    obk = nc.dram_tensor("obk", [128, 128], F32, kind="ExternalInput").ap()
    expf_h = nc.dram_tensor("expf", [16, 1024], F32, kind="ExternalInput").ap()
    bout_h = nc.dram_tensor("bout", [128, 8], F32, kind="ExternalInput").ap()
    b1_h = nc.dram_tensor("b1", [128, 32], F32, kind="ExternalInput").ap()
    b2_h = nc.dram_tensor("b2", [128, 8], F32, kind="ExternalInput").ap()
    outT = nc.dram_tensor("outT", [D, T], F32, kind="ExternalOutput").ap()

    xT_r = xT.rearrange("(dc p) t -> p dc t", p=128)
    wqkvT_r = wqkvT.rearrange("(dc p) o -> p dc o", p=128)
    woutT_r = woutT.rearrange("(dc p) o -> p dc o", p=128)
    w1_r = w1.rearrange("(dc p) i -> p dc i", p=128)
    w2_r = w2.rearrange("(ic p) o -> p ic o", p=128)
    outT_r = outT.rearrange("(j p) t -> p j t", p=128)

    P = Plan()
    es = ExitStack()
    with es:
        sbuf = lambda n, sh, dt: es.enter_context(nc.sbuf_tensor(n, sh, dt))
        psumt = lambda n, sh: es.enter_context(nc.psum_tensor(n, sh, F32))

        cst = sbuf("cst", [128, 8], F32)
        bout = sbuf("bout_sb", [128, 8], F32)
        b1s = sbuf("b1_sb", [128, 32], F32)
        b2s = sbuf("b2_sb", [128, 8], F32)
        rf2b = sbuf("rf2_bf", [128, 128], BF16)
        obkb = sbuf("obk_bf", [128, 128], BF16)
        ones = sbuf("ones_bf", [128, 128], BF16)
        onec = sbuf("onec_bf", [128, 8], BF16)
        expfb = sbuf("expf_bf", [16, 1024], BF16)
        rE = sbuf("rE", [16, 2048], BF16)
        kvs = sbuf("KV_sb", [128, 1024], F32)
        ssb = sbuf("s_sb", [128, 8], F32)
        scols = sbuf("s_cols", [128, 144], BF16)
        qt = sbuf("qt_sb", [128, 8 * T], BF16)
        kvb = sbuf("KV_bf", [128, 1024], BF16)
        rD = sbuf("rD", [16, 512], F32)
        rRb = [sbuf(f"rRb{i}", [16, 512], BF16) for i in range(2)]

        PA = psumt("PA", [128, 1024])
        PB = psumt("PB", [128, 1024])
        PC = psumt("PC", [128, 1024])
        PD = psumt("PD", [128, 1024])

        semh = {s: es.enter_context(nc.semaphore(s)) for s in P.sems}

        def mset(buf_ap, val, key):
            P.emit("dve", lambda v, a=buf_ap, x=val: v.memset(a, x), writes=[key])

        mset(cst[:, 0:1], 1.0 / 1024.0, "cst")
        for i, v in enumerate([1e-5, 0.125, math.log(1e-6), 0.5, -0.5]):
            P.emit("dve", lambda vv, a=cst[:, i + 1:i + 2], x=v: vv.memset(a, x), writes=["cst"])
        mset(ones[:], 1.0, "ones")
        mset(onec[:], 1.0, "onec")
        mset(kvs[:], 0.0, "KVs")
        mset(ssb[:], 0.0, "ssb")
        mset(scols[:], 0.0, "scols")
        mset(kvb[:], 0.0, "KVb")

        def dma(eng, out_ap, in_ap, wkey, rkeys=()):
            wks = [wkey] if isinstance(wkey, str) else list(wkey)
            P.emit(eng, lambda e, o=out_ap, i=in_ap: e.dma_start(out=o, in_=i),
                   reads=rkeys, writes=wks, dma=True)

        dma("sp", bout[:], bout_h, "bout")
        dma("sp", b1s[:], b1_h, "b1")
        dma("sp", b2s[:], b2_h, "b2")


        def mm_group(pairs, wkeys, rkeys, starts=None, stops=None):
            n = len(pairs)
            if starts is None:
                starts = [i == 0 for i in range(n)]
            if stops is None:
                stops = [i == n - 1 for i in range(n)]

            def fn(t, pairs=pairs, starts=starts, stops=stops):
                ins = None
                for (lh, rh, oa), st, sp in zip(pairs, starts, stops):
                    ins = t.matmul(oa, lh, rh, start=st, stop=sp)
                return ins
            P.emit("pe", fn, reads=rkeys, writes=wkeys)

        es1 = ExitStack()
        sb1 = lambda n, sh, dt: es1.enter_context(nc.sbuf_tensor(n, sh, dt))
        rnq = sb1("rnq", [16, 2048], BF16)
        wqb = sb1("wq_bf", [128, 8 * 1024], BF16)
        wkb = sb1("wk_bf", [128, 8 * 1024], BF16)
        wvb = sb1("wv_bf", [128, 8 * 1024], BF16)
        xin = [sb1(f"xin{i}", [128, 8 * C1], BF16) for i in range(2)]
        ubuf = [sb1(f"u{i}", [128, 8 * C1], BF16) for i in range(2)]
        sqb = sb1("sqb", [128, 8 * C1], BF16)
        q_c = sb1("q_c", [128, 8 * C1], BF16)
        k_c = sb1("k_c", [128, 8 * C1], BF16)
        kpre = [sb1(f"kpre{i}", [128, 1024], F32) for i in range(2)]
        kp_c = sb1("kp_c", [128, 4 * 1024], BF16)
        v_c = sb1("v_c", [128, 4 * 1024], BF16)
        nrmk = sb1("nrmk", [128, 4 * 16], F32)
        nrmk2 = sb1("nrmk2", [128, 4 * 16], F32)
        Mtb = [sb1(f"Mtb{i}", [128, C1], BF16) for i in range(2)]
        Rt = [sb1(f"Rt{i}", [128, C1], BF16) for i in range(2)]
        Vt = sb1("Vt", [128, C1], F32)
        msq = sb1("msq", [128, C1], F32)

        def p1_dma_x(c):
            xb = xin[c % 2]
            wk = ([f"xin{c % 2}"] + (["xin0h0", "xin0h1"] if c == 2 else []))
            dma("pool", xb.ap().rearrange("p (dc t) -> p dc t", dc=8),
                xT_r[:, :, c * C1:(c + 1) * C1], wk)

        def p1_xsq(c):
            xb, xk = xin[c % 2], f"xin{c % 2}"
            ub, uk = ubuf[c % 2], f"u{c % 2}"
            P.emit("dve", lambda g, o=ub[:], a=xb[:]: g.tensor_mul(o, a, a),
                   reads=[xk], writes=[uk])

        def p1_ln_a(c):
            xb, xk = xin[c % 2], f"xin{c % 2}"
            ub, uk = ubuf[c % 2], f"u{c % 2}"
            mt = Mtb[c % 2]
            mtk = f"Mt{c % 2}"
            mm_group([(ones[:], xb[:, dc * C1:(dc + 1) * C1], PA[:, 0:512]) for dc in range(8)],
                     ["PA0"], [xk, "ones"])
            mm_group([(ones[:], ub[:, dc * C1:(dc + 1) * C1], PA[:, 512:1024]) for dc in range(8)],
                     ["PA1"], [uk, "ones"])
            P.emit("act", lambda s, o=mt[:]: s.mul(o, PA[:, 0:512], cst[:, 0:1]),
                   reads=["PA0", "cst"], writes=[mtk])

        def p1_ln_b(c):
            xb, xk = xin[c % 2], f"xin{c % 2}"
            ub, uk = ubuf[c % 2], f"u{c % 2}"
            mt, rt = Mtb[c % 2], Rt[c % 2]
            mtk, rtk = f"Mt{c % 2}", f"Rt{c % 2}"
            P.emit("dve", lambda v, m=mt[:]: v.tensor_mul(msq[:], m, m), reads=[mtk], writes=["msq"])
            P.emit("dve", lambda v: v.scalar_tensor_tensor(
                out=Vt[:], in0=PA[:, 512:1024], scalar=cst[:, 0:1], in1=msq[:],
                op0=OP.mult, op1=OP.subtract), reads=["PA1", "msq", "cst"], writes=["Vt"])
            P.emit("act", lambda s: s.activation(out=msq[:], in_=Vt[:], func=AF.Ln, bias=cst[:, 1:2]),
                   reads=["Vt", "cst"], writes=["msq"])
            P.emit("act", lambda s, o=rt[:]: s.activation(out=o, in_=msq[:], func=AF.Exp, scale=cst[:, 5:6]),
                   reads=["msq", "cst"], writes=[rtk])
            P.emit("dve", lambda v, xb=xb, m=mt: v.tensor_sub(
                xb.ap().rearrange("p (dc t) -> p dc t", dc=8),
                xb.ap().rearrange("p (dc t) -> p dc t", dc=8),
                m[:, None, :].broadcast_to([128, 8, C1])),
                reads=[xk, mtk], writes=[xk])
            P.emit("dve", lambda v, xb=xb, ub=ub, r=rt: v.tensor_mul(
                ub.ap().rearrange("p (dc t) -> p dc t", dc=8),
                xb.ap().rearrange("p (dc t) -> p dc t", dc=8),
                r[:, None, :].broadcast_to([128, 8, C1])),
                reads=[xk, rtk], writes=[uk])

        def p1_qk_prod(c, wgt, wk_, dst, dstk, ukf=None):
            ub, uk = ubuf[c % 2], f"u{c % 2}"
            for j in range(8):
                pk = "PB0" if j % 2 == 0 else "PB1"
                po = PB[:, 0:512] if j % 2 == 0 else PB[:, 512:1024]
                wk = wk_ + ("A" if j < 4 else "B")
                if ukf is None:
                    mm_group([(wgt[:, dc * 1024 + j * 128: dc * 1024 + (j + 1) * 128],
                               ub[:, dc * C1:(dc + 1) * C1], po) for dc in range(8)],
                             [pk], [uk, wk])
                else:
                    for dc in range(8):
                        mm_group([(wgt[:, dc * 1024 + j * 128: dc * 1024 + (j + 1) * 128],
                                   ub[:, dc * C1:(dc + 1) * C1], po)],
                                 [pk], [ukf(dc), wk],
                                 starts=[dc == 0], stops=[dc == 7])
                P.emit("act", lambda s, o=dst[:, j * C1:(j + 1) * C1], i=po: s.copy(out=o, in_=i),
                       reads=[pk], writes=[dstk + str(j)])

        def p1_sq(dst, dstk, eng):
            for j in range(8):
                if eng == "act":
                    P.emit("act", lambda g, o=sqb[:, j * C1:(j + 1) * C1],
                           a=dst[:, j * C1:(j + 1) * C1]: g.activation(out=o, in_=a, func=AF.Square),
                           reads=[dstk + str(j)], writes=["sqb"])
                else:
                    P.emit(eng, lambda g, o=sqb[:, j * C1:(j + 1) * C1],
                           a=dst[:, j * C1:(j + 1) * C1]: g.tensor_mul(o, a, a),
                           reads=[dstk + str(j)], writes=["sqb"])

        def p1_chunk(c):
            if c + 1 < NC1:
                p1_xsq(c + 1)
            p1_qk_prod(c, wqb, "wq", q_c, "q_c",
                       ukf=(lambda dc: f"u0d{dc}") if c == 0 else None)
            p1_sq(q_c, "q_c", "dve")
            if c + 1 < NC1:
                p1_ln_a(c + 1)
                p1_ln_b(c + 1)
            p1_qk_prod(c, wkb, "wk", k_c, "k_c")
            mm_group([(obkb[:, 16 * j:16 * (j + 1)], sqb[:, j * C1:(j + 1) * C1],
                       PA[0:16, 0:512]) for j in range(8)],
                     ["PA0"], ["sqb", "obk"])
            P.emit("act", lambda s, o=rnq[:, c * C1:(c + 1) * C1]: s.copy(out=o, in_=PA[0:16, 0:512]),
                   reads=["PA0"], writes=["rows_nq"])
            p1_sq(k_c, "k_c", "dve")
            projrot = [("PC0", PC[:, 0:512]), ("PC1", PC[:, 512:1024]), ("PA0", PA[:, 0:512])]
            for p in range(8):
                pk, po = projrot[p % 3]
                mm_group([(rf2b[:], q_c[:, p * C1:(p + 1) * C1], po)], [pk], [f"q_c{p}", "rf2"])
                P.emit("act", lambda s, o=qt[:, c * 4096 + p * 512: c * 4096 + (p + 1) * 512], i=po:
                       s.activation(out=o, in_=i, func=AF.Exp, scale=cst[:, 2:3]),
                       reads=[pk, "cst"], writes=[f"qt{c}"])
            for s4 in range(4):
                nkk = "PA1" if s4 % 2 == 0 else "PD0"
                nkp = (PA[:, 512 + s4 * 16:512 + (s4 + 1) * 16] if s4 % 2 == 0
                       else PD[:, s4 * 16:(s4 + 1) * 16])
                mm_group([(sqb[:, j * C1 + s4 * 128: j * C1 + (s4 + 1) * 128],
                           obkb[:, 16 * j:16 * (j + 1)],
                           nkp) for j in range(8)],
                         [nkk], ["sqb", "obk"])
                P.emit("act", lambda s, o=nrmk[:, s4 * 16:(s4 + 1) * 16],
                       i=nkp:
                       s.activation(out=o, in_=i, func=AF.Ln),
                       reads=[nkk], writes=[f"nk{s4}"])
                P.emit("act", lambda s, o=nrmk2[:, s4 * 16:(s4 + 1) * 16],
                       i=nrmk[:, s4 * 16:(s4 + 1) * 16]:
                       s.activation(out=o, in_=i, func=AF.Exp, scale=cst[:, 4:5]),
                       reads=[f"nk{s4}", "cst"], writes=[f"nk2_{s4}"])
            sl = slice(c * C1, (c + 1) * C1)
            P.emit("act", lambda s, sl=sl: s.activation(out=rE[:, sl], in_=rnq[:, sl], func=AF.Ln),
                   reads=["rows_nq"], writes=["rows_E"])
            P.emit("act", lambda s, sl=sl: s.activation(out=rnq[:, sl], in_=rE[:, sl], func=AF.Exp,
                                                        scale=cst[0:16, 4:5]),
                   reads=["rows_E", "cst"], writes=["rows_nq"])
            P.emit("act", lambda s, sl=sl: s.activation(out=rE[:, sl], in_=rnq[:, sl], func=AF.Exp,
                                                        bias=cst[0:16, 3:4]),
                   reads=["rows_nq", "cst"], writes=["rows_E"])
            for s4 in range(4):
                PX = PC if s4 % 2 == 0 else PB
                pkx = ["PC0", "PC1"] if s4 % 2 == 0 else ["PB0", "PB1"]
                mm_group([(k_c[:, p * C1 + s4 * 128: p * C1 + (s4 + 1) * 128],
                           rf2b[:], PX[:, p * 128:(p + 1) * 128]) for p in range(8)],
                         pkx,
                         [f"k_c{p}" for p in range(8)] + ["rf2"],
                         starts=[True] * 8, stops=[True] * 8)
                kb = kpre[s4 % 2]
                P.emit("dve", lambda v, o=kb, s4=s4, PX=PX: v.scalar_tensor_tensor(
                    out=o.ap().rearrange("p (a b f) -> p a b f", a=8, b=2),
                    in0=PX.ap().rearrange("p (a b f) -> p a b f", a=8, b=2),
                    scalar=cst[:, 2:3],
                    in1=nrmk2[:, s4 * 16:(s4 + 1) * 16].rearrange("p (a b) -> p a b", a=8)[:, :, :, None]
                        .broadcast_to([128, 8, 2, 64]),
                    op0=OP.mult, op1=OP.subtract),
                    reads=pkx + [f"nk2_{s4}", "cst"], writes=[f"kpre{s4 % 2}"])
                P.emit("act", lambda s, o=kp_c[:, s4 * 1024:(s4 + 1) * 1024], i=kb[:]:
                       s.activation(out=o, in_=i, func=AF.Exp),
                       reads=[f"kpre{s4 % 2}"], writes=[f"kp{s4}"])
            ub, uk = ubuf[c % 2], f"u{c % 2}"
            for s4 in range(4):
                PX = PC if s4 % 2 == 0 else PB
                pkx = ["PC0", "PC1"] if s4 % 2 == 0 else ["PB0", "PB1"]
                prs, sts, sps = [], [], []
                for dc in range(8):
                    for r in range(2):
                        prs.append((ub[:, dc * C1 + s4 * 128: dc * C1 + (s4 + 1) * 128],
                                    wvb[:, dc * 1024 + r * 512: dc * 1024 + (r + 1) * 512],
                                    PX[:, r * 512:(r + 1) * 512]))
                        sts.append(dc == 0)
                        sps.append(dc == 7)
                mm_group(prs, pkx, [uk, "wv"], starts=sts, stops=sps)
                P.emit("act", lambda s, o=v_c[:, s4 * 1024:(s4 + 1) * 1024], i=PX[:]: s.copy(out=o, in_=i),
                       reads=pkx, writes=[f"v{s4}"])
            kvprs, kvst, kvsp = [], [], []
            sprs, sst, ssp = [], [], []
            for p in range(8):
                for s4 in range(4):
                    kvprs.append((kp_c[:, s4 * 1024 + p * 128: s4 * 1024 + (p + 1) * 128],
                                  v_c[:, s4 * 1024 + p * 128: s4 * 1024 + (p + 1) * 128],
                                  PD[:, p * 128:(p + 1) * 128]))
                    kvst.append(s4 == 0)
                    kvsp.append(s4 == 3)
                    sprs.append((kp_c[:, s4 * 1024 + p * 128: s4 * 1024 + (p + 1) * 128],
                                 onec[:, 0:1], PA[:, 512 + p:512 + p + 1]))
                    sst.append(s4 == 0)
                    ssp.append(s4 == 3)
            mm_group(kvprs, ["PD0", "PD1"],
                     [f"kp{s}" for s in range(4)] + [f"v{s}" for s in range(4)],
                     starts=kvst, stops=kvsp)
            P.emit("dve", lambda v: v.tensor_add(kvs[:], kvs[:], PD[:]),
                   reads=["PD0", "PD1", "KVs"], writes=["KVs"])
            mm_group(sprs, ["PA1"],
                     [f"kp{s}" for s in range(4)] + ["onec"],
                     starts=sst, stops=ssp)
            P.emit("dve", lambda v: v.tensor_add(ssb[:], ssb[:], PA[:, 512:520]),
                   reads=["PA1", "ssb"], writes=["ssb"])

        dma("pool", xin[0].ap().rearrange("p (dc t) -> p dc t", dc=8)[:, 0:4],
            xT_r[:, 0:4, 0:C1], "xin0h0")
        dma("pool", xin[0].ap().rearrange("p (dc t) -> p dc t", dc=8)[:, 4:8],
            xT_r[:, 4:8, 0:C1], "xin0h1")
        dma("pool", wqb.ap().rearrange("p (dc o) -> p dc o", dc=8)[:, :, 0:512],
            wqkvT_r[:, :, 0:512], "wqA")
        dma("pool", wqb.ap().rearrange("p (dc o) -> p dc o", dc=8)[:, :, 512:1024],
            wqkvT_r[:, :, 512:1024], "wqB")
        dma("pool", rf2b[:], rf2, "rf2")
        dma("pool", obkb[:], obk, "obk")
        dma("pool", wkb.ap().rearrange("p (dc o) -> p dc o", dc=8)[:, :, 0:512],
            wqkvT_r[:, :, 1024:1536], "wkA")
        dma("pool", wkb.ap().rearrange("p (dc o) -> p dc o", dc=8)[:, :, 512:1024],
            wqkvT_r[:, :, 1536:2048], "wkB")
        p1_dma_x(1)
        dma("pool", expfb[:], expf_h, "expf")
        dma("pool", wvb.ap().rearrange("p (dc o) -> p dc o", dc=8), wqkvT_r[:, :, 2048:3072], "wv")
        ub0, xb0 = ubuf[0], xin[0]
        mt0, rt0 = Mtb[0], Rt[0]
        for h in range(2):
            mm_group([(ones[:], xb0[:, dc * C1:(dc + 1) * C1], PA[:, 0:512])
                      for dc in range(4 * h, 4 * h + 4)],
                     ["PA0"], [f"xin0h{h}", "ones"],
                     starts=[h == 0 and dc == 0 for dc in range(4)],
                     stops=[h == 1 and dc == 3 for dc in range(4)])
        for h in range(2):
            P.emit("dve", lambda g, h=h,
                   o=ub0.ap().rearrange("p (dc t) -> p dc t", dc=8)[:, 4 * h:4 * h + 4],
                   a=xb0.ap().rearrange("p (dc t) -> p dc t", dc=8)[:, 4 * h:4 * h + 4]:
                   g.tensor_mul(o, a, a),
                   reads=[f"xin0h{h}"], writes=[f"u0h{h}"])
        for h in range(2):
            mm_group([(ones[:], ub0[:, dc * C1:(dc + 1) * C1], PA[:, 512:1024])
                      for dc in range(4 * h, 4 * h + 4)],
                     ["PA1"], [f"u0h{h}", "ones"],
                     starts=[h == 0 and dc == 0 for dc in range(4)],
                     stops=[h == 1 and dc == 3 for dc in range(4)])
        P.emit("act", lambda s, o=mt0[:]: s.mul(o, PA[:, 0:512], cst[:, 0:1]),
               reads=["PA0", "cst"], writes=["Mt0"])
        P.emit("dve", lambda v, m=mt0[:]: v.tensor_mul(msq[:], m, m), reads=["Mt0"], writes=["msq"])
        P.emit("dve", lambda v: v.scalar_tensor_tensor(
            out=Vt[:], in0=PA[:, 512:1024], scalar=cst[:, 0:1], in1=msq[:],
            op0=OP.mult, op1=OP.subtract), reads=["PA1", "msq", "cst"], writes=["Vt"])
        P.emit("act", lambda s: s.activation(out=msq[:], in_=Vt[:], func=AF.Ln, bias=cst[:, 1:2]),
               reads=["Vt", "cst"], writes=["msq"])
        P.emit("act", lambda s, o=rt0[:]: s.activation(out=o, in_=msq[:], func=AF.Exp, scale=cst[:, 5:6]),
               reads=["msq", "cst"], writes=["Rt0"])
        for dc in range(8):
            P.emit("dve", lambda v, o=xb0[:, dc * C1:(dc + 1) * C1], m=mt0[:]:
                   v.tensor_sub(o, o, m),
                   reads=[f"xin0h{dc // 4}", "Mt0"],
                   writes=[f"x0d{dc}"] + (["xin0"] if dc == 7 else []))
        for dc in range(8):
            P.emit("dve", lambda v, o=ub0[:, dc * C1:(dc + 1) * C1],
                   a=xb0[:, dc * C1:(dc + 1) * C1], r=rt0[:]:
                   v.tensor_mul(o, a, r),
                   reads=[f"x0d{dc}", f"xin0h{dc // 4}", "Rt0"],
                   writes=[f"u0d{dc}"] + (["u0"] if dc == 7 else []))
        for c in range(NC1):
            if c + 2 < NC1:
                p1_dma_x(c + 2)
            p1_chunk(c)

        sc_v = scols.ap().rearrange("p (g c) -> p g c", c=18)
        P.emit("act", lambda s: s.copy(out=sc_v[0:64, :, 0:1], in_=ssb[0:64, :, None]),
               reads=["ssb", "scols"], writes=["scols"])
        P.emit("act", lambda s: s.copy(out=sc_v[64:128, :, 1:2], in_=ssb[64:128, :, None]),
               reads=["ssb", "scols"], writes=["scols"])

        for p in range(8):
            P.emit("act", lambda s, p=p: s.copy(out=kvb[0:64, p * 128:p * 128 + 64],
                                                in_=kvs[0:64, p * 128:p * 128 + 64]),
                   reads=["KVs", "KVb"], writes=["KVb"])
            P.emit("act", lambda s, p=p: s.copy(out=kvb[64:128, p * 128 + 64:(p + 1) * 128],
                                                in_=kvs[64:128, p * 128 + 64:(p + 1) * 128]),
                   reads=["KVs", "KVb"], writes=["KVb"])

        Vt2 = kvs[:, 0:512]
        ms2 = kvs[:, 512:1024]
        rR = kvs[0:16, 0:512]

        def denom(i, pk="PA0", split=False):
            t0 = i * 512
            rrb = rRb[i % 2]
            pw = PA[0:16, 0:512] if pk == "PA0" else PA[0:16, 512:1024]
            mm_group([(scols[:, 16 * p:16 * (p + 1)], qt[:, i * 4096 + p * 512: i * 4096 + (p + 1) * 512],
                       pw) for p in range(8)],
                     [pk], ["scols", f"qt{i}"])
            P.emit("dve", lambda v, t0=t0, pw=pw: v.tensor_add(rD[:], pw,
                                                               rE[:, t0:t0 + 512]),
                   reads=[pk, "rows_E"], writes=["r2D"])
            if split:
                for h in range(2):
                    P.emit("dve", lambda v, h=h: v.reciprocal(
                        out=rR[:, h * 256:(h + 1) * 256], in_=rD[:, h * 256:(h + 1) * 256]),
                        reads=["r2D"], writes=["r2R", "KVs"])
                    P.emit("act", lambda s, rrb=rrb, h=h: s.copy(
                        out=rrb[:, h * 256:(h + 1) * 256], in_=rR[:, h * 256:(h + 1) * 256]),
                        reads=["r2R"], writes=[f"r2Rb{i % 2}h{h}"])
            else:
                P.emit("dve", lambda v: v.reciprocal(out=rR, in_=rD[:]),
                       reads=["r2D"], writes=["r2R", "KVs"])
                P.emit("act", lambda s, rrb=rrb: s.copy(out=rrb[:], in_=rR),
                       reads=["r2R"], writes=[f"r2Rb{i % 2}h0", f"r2Rb{i % 2}h1"])

        es1.close()
        P.barrier(compute_only=True, skip=("pe",))

        denom(0, split=True)
        denom(1, pk="PA1", split=True)

        es2 = ExitStack()
        sb2 = lambda n, sh, dt: es2.enter_context(nc.sbuf_tensor(n, sh, dt))
        x2 = sb2("x2_sb", [128, 8 * T], F32)
        u2 = sb2("u2_sb", [128, 8 * T], BF16)
        wob = sb2("woutT_bf", [128, 8 * 1024], BF16)
        dma("pool", wob.ap().rearrange("p (dc o) -> p dc o", dc=8), woutT_r[:, :, :], "wout")

        es2a = ExitStack()
        sb2a = lambda n, sh, dt: es2a.enter_context(nc.sbuf_tensor(n, sh, dt))
        xin2 = [sb2a(f"xin2_{i}", [128, 8 * C2], F32) for i in range(2)]
        attn = [sb2a(f"attn{i}", [128, 8 * 512], BF16) for i in range(2)]
        xb2 = sb2a("xb2", [128, 8 * 512], BF16)
        Mt2b = sb2a("Mt2b", [128, 512], BF16)
        Rt2b = sb2a("Rt2b", [128, 512], BF16)
        Rsb = [sb2a(f"Rsb{i}", [128, 512], BF16) for i in range(2)]

        W1g = [qt[:, 0:4096], qt[:, 4096:8192]]
        W2g = [qt[:, 8192:12288], qt[:, 12288:16384]]

        XB2K = [f"xb2_{j}" for j in range(8)]

        def ln2_copy(i, j):
            P.emit("act", lambda g, i=i, j=j: g.copy(
                out=xb2[:, j * 512:(j + 1) * 512],
                in_=x2[:, j * T + i * 512: j * T + (i + 1) * 512]),
                reads=[f"x2p{i}_{j}"], writes=[f"xb2_{j}"])

        def ln2_rest(i):
            u2b_ap = u2.ap().rearrange("p (i j t) -> p i j t", i=4, j=8)[:, i]
            uk = "u2a" if i < 2 else "u2b"
            mm_group([(ones[:], xb2[:, j * 512:(j + 1) * 512], PD[:, 0:512]) for j in range(8)],
                     ["PD0"], XB2K + ["ones"])
            P.emit("dve", lambda g, o=u2b_ap: g.tensor_mul(
                o, xb2.ap().rearrange("p (j t) -> p j t", j=8),
                xb2.ap().rearrange("p (j t) -> p j t", j=8)),
                reads=XB2K, writes=[uk])
            mm_group([(ones[:], u2[:, i * 4096 + j * 512: i * 4096 + (j + 1) * 512],
                       PD[:, 512:1024]) for j in range(8)],
                     ["PD1"], [uk, "ones"])
            P.emit("act", lambda s: s.mul(Mt2b[:], PD[:, 0:512], cst[:, 0:1]),
                   reads=["PD0", "cst"], writes=["Mt2b"])
            P.emit("dve", lambda v: v.tensor_mul(ms2, Mt2b[:], Mt2b[:]), reads=["Mt2b"], writes=["ms2"])
            P.emit("dve", lambda v: v.scalar_tensor_tensor(
                out=Vt2, in0=PD[:, 512:1024], scalar=cst[:, 0:1], in1=ms2,
                op0=OP.mult, op1=OP.subtract),
                reads=["PD1", "ms2", "cst", "r2R", "r2Rb0", "r2Rb1"], writes=["Vt2"])
            P.emit("act", lambda s: s.activation(out=ms2, in_=Vt2, func=AF.Ln, bias=cst[:, 1:2]),
                   reads=["Vt2", "cst"], writes=["ms2"])
            P.emit("act", lambda s: s.activation(out=Rt2b[:], in_=ms2, func=AF.Exp, scale=cst[:, 5:6]),
                   reads=["ms2", "cst"], writes=["Rt2"])
            P.emit("dve", lambda v: v.tensor_sub(
                xb2.ap().rearrange("p (j t) -> p j t", j=8),
                xb2.ap().rearrange("p (j t) -> p j t", j=8),
                Mt2b[:, None, :].broadcast_to([128, 8, 512])),
                reads=["Mt2b"] + XB2K, writes=XB2K)
            P.emit("dve", lambda v, o=u2b_ap: v.tensor_mul(
                o, xb2.ap().rearrange("p (j t) -> p j t", j=8),
                Rt2b[:, None, :].broadcast_to([128, 8, 512])),
                reads=["Rt2"] + XB2K, writes=[uk])

        def wout_j(i, j):
            ab = i % 2
            pk = "PC0" if j % 2 == 0 else "PC1"
            po = PC[:, 0:512] if j % 2 == 0 else PC[:, 512:1024]
            if j % 2 == 1 and j % 4 == 3:
                pass
            pt = PC if j % 2 == 0 else PD
            pks = ["PC0", "PC1"] if j % 2 == 0 else ["PD0", "PD1"]
            bank = (j % 4)
            src = [PC[:, 0:512], PC[:, 512:1024], PD[:, 0:512], PD[:, 512:1024]][bank]
            key = ["PC0", "PC1", "PD0", "PD1"][bank]
            mm_group([(wob[:, dc * 1024 + j * 128: dc * 1024 + (j + 1) * 128],
                       attn[ab][:, dc * 512:(dc + 1) * 512], src) for dc in range(8)],
                     [key], [f"at{ab}_{dc}" for dc in range(8)] + ["wout"])
            for cc in range(2):
                cg = 2 * i + cc
                P.emit("dve", lambda v, j=j, cg=cg, cc=cc, i_=src, xi=xin2[cg % 2]:
                       v.scalar_tensor_tensor(
                           out=x2[:, j * T + cg * C2: j * T + (cg + 1) * C2],
                           in0=i_[:, cc * C2:(cc + 1) * C2],
                           scalar=bout[:, j:j + 1], in1=xi[:, j * C2:(j + 1) * C2],
                           op0=OP.add, op1=OP.add),
                       reads=[key, "bout", f"xin2_{cg % 2}"], writes=[f"x2p{i}_{j}"])
            ln2_copy(i, j)

        dma("sp", xin2[0].ap().rearrange("p (j t) -> p j t", j=8), xT_r[:, :, 0:C2], "xin2_0")
        dma("sp", xin2[1].ap().rearrange("p (j t) -> p j t", j=8), xT_r[:, :, C2:2 * C2], "xin2_1")

        for i in range(NC2 // 2):
            t0 = i * 512
            ab = i % 2
            A = attn[ab]
            rrb = rRb[i % 2]
            for p in range(8):
                pkR = "PA0" if p % 2 == 0 else "PA1"
                poR = PA[:, 0:512] if p % 2 == 0 else PA[:, 512:1024]
                if i < 2:
                    for h in range(2):
                        mm_group([(expfb[:, p * 128:(p + 1) * 128],
                                   rrb[:, h * 256:(h + 1) * 256],
                                   poR[:, h * 256:(h + 1) * 256])],
                                 [pkR], [f"r2Rb{i % 2}h{h}", "expf"])
                else:
                    mm_group([(expfb[:, p * 128:(p + 1) * 128], rrb[:], poR)],
                             [pkR], [f"r2Rb{i % 2}h0", f"r2Rb{i % 2}h1", "expf"])
                rb = Rsb[p % 2]
                P.emit("act", lambda s, o=rb[:], i_=poR: s.copy(out=o, in_=i_),
                       reads=[pkR], writes=[f"Rsb{p % 2}"])
                pkN = "PB0" if p % 2 == 0 else "PB1"
                po = PB[:, 0:512] if p % 2 == 0 else PB[:, 512:1024]
                mm_group([(kvb[:, p * 128:(p + 1) * 128],
                           qt[:, i * 4096 + p * 512: i * 4096 + (p + 1) * 512], po)],
                         [pkN], ["KVb", f"qt{i}"])
                P.emit("dve", lambda v, o=A[:, p * 512:(p + 1) * 512], i_=po, r=rb[:]:
                       v.tensor_mul(o, i_, r),
                       reads=[pkN, f"Rsb{p % 2}"], writes=[f"at{ab}_{p}"])

            if i + 2 < NC2 // 2:
                denom(i + 2)

            if i > 0:
                for j in range(8):
                    wout_j(i - 1, j)
                ln2_rest(i - 1)

            if i == 0:
                P.emit("pool", lambda e: e.dma_start(out=W1g[0].rearrange("p (dc i) -> p dc i", dc=8),
                                                     in_=w1_r[:, :, 0:GI]),
                       writes=["W1g0", "qt0"], dma=True)
            elif i == 1:
                P.emit("pool", lambda e: e.dma_start(out=W1g[1].rearrange("p (dc i) -> p dc i", dc=8),
                                                     in_=w1_r[:, :, GI:2 * GI]),
                       writes=["W1g1", "qt1"], dma=True)
            elif i == 2:
                P.emit("pool", lambda e: e.dma_start(out=W2g[0].rearrange("p (ic o) -> p ic o", ic=4),
                                                     in_=w2_r[:, 0:4, :]),
                       writes=["W2g0", "qt2"], dma=True)
            else:
                P.emit("pool", lambda e: e.dma_start(out=W2g[1].rearrange("p (ic o) -> p ic o", ic=4),
                                                     in_=w2_r[:, 4:8, :]),
                       writes=["W2g1", "qt3"], dma=True)

            if i >= 1:
                dma("sp", xin2[0].ap().rearrange("p (j t) -> p j t", j=8),
                    xT_r[:, :, (2 * i) * C2:(2 * i + 1) * C2], "xin2_0")
                dma("sp", xin2[1].ap().rearrange("p (j t) -> p j t", j=8),
                    xT_r[:, :, (2 * i + 1) * C2:(2 * i + 2) * C2], "xin2_1")

        for j in range(8):
            wout_j(NC2 // 2 - 1, j)

        es2a.close()

        h_sb = wob

        es2b = ExitStack()
        sb2b = lambda n, sh, dt: es2b.enter_context(nc.sbuf_tensor(n, sh, dt))
        ostg = [sb2b(f"ostg{i}", [128, TH], F32) for i in range(3)]

        def load_w1(g):
            dma("pool", W1g[g % 2].rearrange("p (dc i) -> p dc i", dc=8),
                w1_r[:, :, g * GI:(g + 1) * GI], f"W1g{g % 2}")

        def load_w2(g):
            dma("pool", W2g[g % 2].rearrange("p (ic o) -> p ic o", ic=4),
                w2_r[:, g * 4:(g + 1) * 4, :], f"W2g{g % 2}")

        ns = 0
        for gp in range(NG // 2):
            for th in range(NTH):
                for gg in range(2):
                    g = 2 * gp + gg
                    buf = g % 2
                    for ic in range(4):
                        prs, sts, sps = [], [], []
                        for dc in range(8):
                            for r in range(2):
                                pr = 2 * th + r
                                prs.append((W1g[buf][:, dc * 512 + ic * 128: dc * 512 + (ic + 1) * 128],
                                            u2[:, pr * 4096 + dc * 512: pr * 4096 + (dc + 1) * 512],
                                            PA[:, r * 512:(r + 1) * 512] if ic % 2 == 0
                                            else PB[:, r * 512:(r + 1) * 512]))
                                sts.append(dc == 0)
                                sps.append(dc == 7)
                        pk = ["PA0", "PA1"] if ic % 2 == 0 else ["PB0", "PB1"]
                        mm_group(prs, pk, ["u2a" if th == 0 else "u2b", f"W1g{buf}"],
                                 starts=sts, stops=sps)
                        src = PA if ic % 2 == 0 else PB
                        P.emit("act", lambda s, o=h_sb[:, gg * 4096 + ic * 1024: gg * 4096 + (ic + 1) * 1024],
                               i=src[:], b=b1s[:, g * 4 + ic:g * 4 + ic + 1]:
                               s.activation(out=o, in_=i, func=AF.Relu, bias=b),
                               reads=pk + ["b1"], writes=[f"h{gg}_{ic}", "wout"])
                    if gp == 0 and th == 0 and gg == 0:
                        ln2_rest(NC2 // 2 - 1)
                    if th == NTH - 1 and g + 2 < NG:
                        load_w1(g + 2)
                for j in range(8):
                    pt, pks = (PC, ["PC0", "PC1"]) if j % 2 == 0 else (PD, ["PD0", "PD1"])
                    prs, sts, sps = [], [], []
                    for icg in range(8):
                        gg2, ic = icg // 4, icg % 4
                        bufw = (2 * gp + gg2) % 2
                        for r in range(2):
                            prs.append((W2g[bufw][:, ic * 1024 + j * 128: ic * 1024 + (j + 1) * 128],
                                        h_sb[:, gg2 * 4096 + ic * 1024 + r * 512:
                                             gg2 * 4096 + ic * 1024 + (r + 1) * 512],
                                        pt[:, r * 512:(r + 1) * 512]))
                            sts.append(icg == 0)
                            sps.append(icg == 7)
                    hkeys = [f"h{gg2}_{ic}" for gg2 in range(2) for ic in range(4)]
                    if j == 0:
                        mm_group(prs[:14], pks, hkeys[:7] + ["W2g0", "W2g1"],
                                 starts=sts[:14], stops=[False] * 14)
                        mm_group(prs[14:], pks, [hkeys[7]],
                                 starts=[False, False], stops=sps[14:])
                    else:
                        mm_group(prs, pks, hkeys + ["W2g0", "W2g1"],
                                 starts=sts, stops=sps)
                    x2k = [f"x2p{2 * th}_{j}", f"x2p{2 * th + 1}_{j}"]
                    if gp < NG // 2 - 1:
                        P.emit("dve", lambda v, j=j, th=th, i=pt[:]: v.tensor_add(
                            x2[:, j * T + th * TH: j * T + (th + 1) * TH],
                            x2[:, j * T + th * TH: j * T + (th + 1) * TH], i),
                            reads=pks + x2k, writes=x2k)
                    else:
                        ob = ostg[ns % 3]
                        nq_ = 4 if ns >= 14 else 1
                        cw = TH // nq_
                        for q in range(nq_):
                            P.emit("dve", lambda v, j=j, th=th, q=q, cw=cw,
                                   i=pt[:, q * cw:(q + 1) * cw], o=ob[:, q * cw:(q + 1) * cw]:
                                   v.scalar_tensor_tensor(
                                       out=o, in0=i, scalar=b2s[:, j:j + 1],
                                       in1=x2[:, j * T + th * TH + q * cw:
                                              j * T + th * TH + (q + 1) * cw],
                                       op0=OP.add, op1=OP.add),
                                   reads=pks + x2k + ["b2"],
                                   writes=[f"ostg{ns % 3}", f"ostg{ns % 3}_{q}"])
                            dma("sp", outT_r[:, j, th * TH + q * cw: th * TH + (q + 1) * cw],
                                ob[:, q * cw:(q + 1) * cw],
                                f"od{ns}_{q}", rkeys=[f"ostg{ns % 3}_{q}"])
                        ns += 1
                if th == NTH - 1 and gp < NG // 2 - 1:
                    load_w2(2 * gp + 2)
                    load_w2(2 * gp + 3)

        es2b.close()
        es2.close()
        P.final_wait("sp", "sp")

        with nc.Block() as block:
            def runner(name):
                def run(eng):
                    for waits, fn, sem in P.ops[name]:
                        for (s, v) in waits:
                            eng.wait_ge(semh[s], v)
                        if fn is not None:
                            ins = fn(eng)
                            ins.then_inc(semh[sem], 1 if sem.startswith("c_") else 16)
                return run
            block.tensor(runner("pe"))
            block.scalar(runner("act"))
            block.vector(runner("dve"))
            block.gpsimd(runner("pool"))
            block.sync(runner("sp"))

    return nc


def kernel(L0_attn_x, L0_attn_w_qkv, L0_attn_w_out, L0_attn_b_out,
           L0_attn_random_features, L0_ffn_W1, L0_ffn_b1, L0_ffn_W2, L0_ffn_b2):
    x = np.asarray(L0_attn_x, dtype=np.float32)
    wqkvT = np.ascontiguousarray(np.asarray(L0_attn_w_qkv, np.float32).T)
    woutT = np.ascontiguousarray(np.asarray(L0_attn_w_out, np.float32).T)
    w1 = np.ascontiguousarray(np.asarray(L0_ffn_W1, np.float32))
    w2 = np.ascontiguousarray(np.asarray(L0_ffn_W2, np.float32))
    rf = np.asarray(L0_attn_random_features, np.float32)
    b_out = np.asarray(L0_attn_b_out, np.float32)
    b1 = np.asarray(L0_ffn_b1, np.float32)
    b2 = np.asarray(L0_ffn_b2, np.float32)

    rf2 = np.zeros((128, 128), np.float32)
    rf2[0:64, 0:64] = rf.T
    rf2[64:128, 64:128] = rf.T
    obk = np.zeros((128, 128), np.float32)
    for p in range(8):
        obk[0:64, 16 * p + 2 * p] = 1.0
        obk[64:128, 16 * p + 2 * p + 1] = 1.0
    expf = np.zeros((16, 1024), np.float32)
    for p in range(8):
        expf[2 * p, p * 128:p * 128 + 64] = 1.0
        expf[2 * p + 1, p * 128 + 64:(p + 1) * 128] = 1.0
    bout_h = np.ascontiguousarray(b_out.reshape(8, 128).T)
    b1_h = np.ascontiguousarray(b1.reshape(32, 128).T)
    b2_h = np.ascontiguousarray(b2.reshape(8, 128).T)

    nc = build_nc()
    in_maps = []
    for c in range(NCORES):
        in_maps.append({
            "xT": np.ascontiguousarray(x[c].T),
            "wqkvT": wqkvT, "woutT": woutT, "w1": w1, "w2": w2,
            "rf2": rf2, "obk": obk, "expf": expf,
            "bout": bout_h, "b1": b1_h, "b2": b2_h,
        })
    res = run_bass_kernel_spmd(nc, in_maps, core_ids=list(range(NCORES)))
    out = np.stack([np.ascontiguousarray(r["outT"].T) for r in res.results], axis=0)
    return out


# version 72
# speedup vs baseline: 1.0011x; 1.0011x over previous
import math
import numpy as np
from contextlib import ExitStack

import concourse.bass as bass
import concourse.mybir as mybir
from concourse.bass_utils import run_bass_kernel_spmd

F32 = mybir.dt.float32
BF16 = mybir.dt.bfloat16
AF = mybir.ActivationFunctionType
OP = mybir.AluOpType

NCORES = 8
D = 1024
T = 2048
INTER = 4096

C1 = 512
NC1 = T // C1
C2 = 256
NC2 = T // C2
TH = 1024
NTH = T // TH
NG = 8
GI = INTER // NG

ENGINES = ("pe", "act", "dve", "pool", "sp")


class Plan:

    def __init__(self):
        self.ops = {e: [] for e in ENGINES}
        self.dma_slots = {"pool": [f"dp{i}" for i in range(8)],
                          "sp": [f"ds{i}" for i in range(12)]}
        self.dma_rr = {"pool": 0, "sp": 0}
        self.sems = (["c_pe", "c_act", "c_dve", "c_pool"]
                     + self.dma_slots["pool"] + self.dma_slots["sp"])
        self.cnt = {s: 0 for s in self.sems}
        self.water = {}
        self.reg = {}

    def _want(self, eng, waits, sem, val):
        if val <= 0:
            return
        if eng == "pe" and sem == "c_pe":
            return
        k = (eng, sem)
        if self.water.get(k, 0) >= val:
            return
        self.water[k] = val
        waits.append((sem, val))

    def emit(self, eng, fn, reads=(), writes=(), dma=False):
        waits = []
        if dma:
            slots = self.dma_slots[eng]
            sem = slots[self.dma_rr[eng] % len(slots)]
            self.dma_rr[eng] += 1
            self._want(eng, waits, sem, self.cnt[sem])
        else:
            sem = "c_" + eng
        for r in reads:
            info = self.reg.get(r)
            if info and info["w"] is not None:
                self._want(eng, waits, *info["w"])
        for w in writes:
            info = self.reg.get(w)
            if info:
                if info["w"] is not None:
                    self._want(eng, waits, *info["w"])
                for rd in info["rs"]:
                    self._want(eng, waits, *rd)
        self.cnt[sem] += 16 if dma else 1
        val = self.cnt[sem]
        self.ops[eng].append((waits, fn, sem))
        for r in reads:
            self.reg.setdefault(r, {"w": None, "rs": []})["rs"].append((sem, val))
        for w in writes:
            self.reg[w] = {"w": (sem, val), "rs": []}

    def barrier(self, compute_only=False, skip=()):
        snap = dict(self.cnt)
        for e in ENGINES:
            if e in skip:
                continue
            waits = []
            for s, v in snap.items():
                if s == "c_" + e:
                    continue
                if compute_only and not s.startswith("c_"):
                    continue
                self._want(e, waits, s, v)
            if waits:
                self.ops[e].append((waits, None, None))

    def final_wait(self, eng, queue):
        waits = [(sem, self.cnt[sem]) for sem in self.dma_slots[queue] if self.cnt[sem] > 0]
        self.ops[eng].append((waits, None, None))


def build_nc():
    nc = bass.Bass("TRN2", target_bir_lowering=False)

    xT = nc.dram_tensor("xT", [D, T], F32, kind="ExternalInput").ap()
    wqkvT = nc.dram_tensor("wqkvT", [D, 3 * D], F32, kind="ExternalInput").ap()
    woutT = nc.dram_tensor("woutT", [D, D], F32, kind="ExternalInput").ap()
    w1 = nc.dram_tensor("w1", [D, INTER], F32, kind="ExternalInput").ap()
    w2 = nc.dram_tensor("w2", [INTER, D], F32, kind="ExternalInput").ap()
    rf2 = nc.dram_tensor("rf2", [128, 128], F32, kind="ExternalInput").ap()
    obk = nc.dram_tensor("obk", [128, 128], F32, kind="ExternalInput").ap()
    expf_h = nc.dram_tensor("expf", [16, 1024], F32, kind="ExternalInput").ap()
    bout_h = nc.dram_tensor("bout", [128, 8], F32, kind="ExternalInput").ap()
    b1_h = nc.dram_tensor("b1", [128, 32], F32, kind="ExternalInput").ap()
    b2_h = nc.dram_tensor("b2", [128, 8], F32, kind="ExternalInput").ap()
    outT = nc.dram_tensor("outT", [D, T], F32, kind="ExternalOutput").ap()

    xT_r = xT.rearrange("(dc p) t -> p dc t", p=128)
    wqkvT_r = wqkvT.rearrange("(dc p) o -> p dc o", p=128)
    woutT_r = woutT.rearrange("(dc p) o -> p dc o", p=128)
    w1_r = w1.rearrange("(dc p) i -> p dc i", p=128)
    w2_r = w2.rearrange("(ic p) o -> p ic o", p=128)
    outT_r = outT.rearrange("(j p) t -> p j t", p=128)

    P = Plan()
    es = ExitStack()
    with es:
        sbuf = lambda n, sh, dt: es.enter_context(nc.sbuf_tensor(n, sh, dt))
        psumt = lambda n, sh: es.enter_context(nc.psum_tensor(n, sh, F32))

        cst = sbuf("cst", [128, 8], F32)
        bout = sbuf("bout_sb", [128, 8], F32)
        b1s = sbuf("b1_sb", [128, 32], F32)
        b2s = sbuf("b2_sb", [128, 8], F32)
        rf2b = sbuf("rf2_bf", [128, 128], BF16)
        obkb = sbuf("obk_bf", [128, 128], BF16)
        ones = sbuf("ones_bf", [128, 128], BF16)
        onec = sbuf("onec_bf", [128, 8], BF16)
        expfb = sbuf("expf_bf", [16, 1024], BF16)
        rE = sbuf("rE", [16, 2048], BF16)
        kvs = sbuf("KV_sb", [128, 1024], F32)
        ssb = sbuf("s_sb", [128, 8], F32)
        scols = sbuf("s_cols", [128, 144], BF16)
        qt = sbuf("qt_sb", [128, 8 * T], BF16)
        kvb = sbuf("KV_bf", [128, 1024], BF16)
        rD = sbuf("rD", [16, 512], F32)
        rRb = [sbuf(f"rRb{i}", [16, 512], BF16) for i in range(2)]

        PA = psumt("PA", [128, 1024])
        PB = psumt("PB", [128, 1024])
        PC = psumt("PC", [128, 1024])
        PD = psumt("PD", [128, 1024])

        semh = {s: es.enter_context(nc.semaphore(s)) for s in P.sems}

        def mset(buf_ap, val, key):
            P.emit("dve", lambda v, a=buf_ap, x=val: v.memset(a, x), writes=[key])

        mset(cst[:, 0:1], 1.0 / 1024.0, "cst")
        for i, v in enumerate([1e-5, 0.125, math.log(1e-6), 0.5, -0.5]):
            P.emit("dve", lambda vv, a=cst[:, i + 1:i + 2], x=v: vv.memset(a, x), writes=["cst"])
        mset(ones[:], 1.0, "ones")
        mset(onec[:], 1.0, "onec")
        mset(kvs[:], 0.0, "KVs")
        mset(ssb[:], 0.0, "ssb")
        mset(scols[:], 0.0, "scols")
        mset(kvb[:], 0.0, "KVb")

        def dma(eng, out_ap, in_ap, wkey, rkeys=()):
            wks = [wkey] if isinstance(wkey, str) else list(wkey)
            P.emit(eng, lambda e, o=out_ap, i=in_ap: e.dma_start(out=o, in_=i),
                   reads=rkeys, writes=wks, dma=True)

        dma("sp", bout[:], bout_h, "bout")
        dma("sp", b1s[:], b1_h, "b1")
        dma("sp", b2s[:], b2_h, "b2")


        def mm_group(pairs, wkeys, rkeys, starts=None, stops=None):
            n = len(pairs)
            if starts is None:
                starts = [i == 0 for i in range(n)]
            if stops is None:
                stops = [i == n - 1 for i in range(n)]

            def fn(t, pairs=pairs, starts=starts, stops=stops):
                ins = None
                for (lh, rh, oa), st, sp in zip(pairs, starts, stops):
                    ins = t.matmul(oa, lh, rh, start=st, stop=sp)
                return ins
            P.emit("pe", fn, reads=rkeys, writes=wkeys)

        es1 = ExitStack()
        sb1 = lambda n, sh, dt: es1.enter_context(nc.sbuf_tensor(n, sh, dt))
        rnq = sb1("rnq", [16, 2048], BF16)
        wqb = sb1("wq_bf", [128, 8 * 1024], BF16)
        wkb = sb1("wk_bf", [128, 8 * 1024], BF16)
        wvb = sb1("wv_bf", [128, 8 * 1024], BF16)
        xin = [sb1(f"xin{i}", [128, 8 * C1], BF16) for i in range(2)]
        ubuf = [sb1(f"u{i}", [128, 8 * C1], BF16) for i in range(2)]
        sqb = sb1("sqb", [128, 8 * C1], BF16)
        q_c = sb1("q_c", [128, 8 * C1], BF16)
        k_c = sb1("k_c", [128, 8 * C1], BF16)
        kpre = [sb1(f"kpre{i}", [128, 1024], F32) for i in range(2)]
        kp_c = sb1("kp_c", [128, 4 * 1024], BF16)
        v_c = sb1("v_c", [128, 4 * 1024], BF16)
        nrmk = sb1("nrmk", [128, 4 * 16], F32)
        nrmk2 = sb1("nrmk2", [128, 4 * 16], F32)
        Mtb = [sb1(f"Mtb{i}", [128, C1], BF16) for i in range(2)]
        Rt = [sb1(f"Rt{i}", [128, C1], BF16) for i in range(2)]
        Vt = sb1("Vt", [128, C1], F32)
        msq = sb1("msq", [128, C1], F32)

        def p1_dma_x(c):
            xb = xin[c % 2]
            wk = ([f"xin{c % 2}"] + (["xin0h0", "xin0h1"] if c == 2 else []))
            dma("pool", xb.ap().rearrange("p (dc t) -> p dc t", dc=8),
                xT_r[:, :, c * C1:(c + 1) * C1], wk)

        def p1_xsq(c):
            xb, xk = xin[c % 2], f"xin{c % 2}"
            ub, uk = ubuf[c % 2], f"u{c % 2}"
            P.emit("dve", lambda g, o=ub[:], a=xb[:]: g.tensor_mul(o, a, a),
                   reads=[xk], writes=[uk])

        def p1_ln_a(c):
            xb, xk = xin[c % 2], f"xin{c % 2}"
            ub, uk = ubuf[c % 2], f"u{c % 2}"
            mt = Mtb[c % 2]
            mtk = f"Mt{c % 2}"
            mm_group([(ones[:], xb[:, dc * C1:(dc + 1) * C1], PA[:, 0:512]) for dc in range(8)],
                     ["PA0"], [xk, "ones"])
            mm_group([(ones[:], ub[:, dc * C1:(dc + 1) * C1], PA[:, 512:1024]) for dc in range(8)],
                     ["PA1"], [uk, "ones"])
            P.emit("act", lambda s, o=mt[:]: s.mul(o, PA[:, 0:512], cst[:, 0:1]),
                   reads=["PA0", "cst"], writes=[mtk])

        def p1_ln_b(c):
            xb, xk = xin[c % 2], f"xin{c % 2}"
            ub, uk = ubuf[c % 2], f"u{c % 2}"
            mt, rt = Mtb[c % 2], Rt[c % 2]
            mtk, rtk = f"Mt{c % 2}", f"Rt{c % 2}"
            P.emit("dve", lambda v, m=mt[:]: v.tensor_mul(msq[:], m, m), reads=[mtk], writes=["msq"])
            P.emit("dve", lambda v: v.scalar_tensor_tensor(
                out=Vt[:], in0=PA[:, 512:1024], scalar=cst[:, 0:1], in1=msq[:],
                op0=OP.mult, op1=OP.subtract), reads=["PA1", "msq", "cst"], writes=["Vt"])
            P.emit("act", lambda s: s.activation(out=msq[:], in_=Vt[:], func=AF.Ln, bias=cst[:, 1:2]),
                   reads=["Vt", "cst"], writes=["msq"])
            P.emit("act", lambda s, o=rt[:]: s.activation(out=o, in_=msq[:], func=AF.Exp, scale=cst[:, 5:6]),
                   reads=["msq", "cst"], writes=[rtk])
            P.emit("dve", lambda v, xb=xb, m=mt: v.tensor_sub(
                xb.ap().rearrange("p (dc t) -> p dc t", dc=8),
                xb.ap().rearrange("p (dc t) -> p dc t", dc=8),
                m[:, None, :].broadcast_to([128, 8, C1])),
                reads=[xk, mtk], writes=[xk])
            P.emit("dve", lambda v, xb=xb, ub=ub, r=rt: v.tensor_mul(
                ub.ap().rearrange("p (dc t) -> p dc t", dc=8),
                xb.ap().rearrange("p (dc t) -> p dc t", dc=8),
                r[:, None, :].broadcast_to([128, 8, C1])),
                reads=[xk, rtk], writes=[uk])

        def p1_qk_prod(c, wgt, wk_, dst, dstk, ukf=None):
            ub, uk = ubuf[c % 2], f"u{c % 2}"
            for j in range(8):
                pk = "PB0" if j % 2 == 0 else "PB1"
                po = PB[:, 0:512] if j % 2 == 0 else PB[:, 512:1024]
                wk = wk_ + ("A" if j < 4 else "B")
                if ukf is None:
                    mm_group([(wgt[:, dc * 1024 + j * 128: dc * 1024 + (j + 1) * 128],
                               ub[:, dc * C1:(dc + 1) * C1], po) for dc in range(8)],
                             [pk], [uk, wk])
                else:
                    for dc in range(8):
                        mm_group([(wgt[:, dc * 1024 + j * 128: dc * 1024 + (j + 1) * 128],
                                   ub[:, dc * C1:(dc + 1) * C1], po)],
                                 [pk], [ukf(dc), wk],
                                 starts=[dc == 0], stops=[dc == 7])
                P.emit("act", lambda s, o=dst[:, j * C1:(j + 1) * C1], i=po: s.copy(out=o, in_=i),
                       reads=[pk], writes=[dstk + str(j)])

        def p1_sq(dst, dstk, eng):
            for j in range(8):
                if eng == "act":
                    P.emit("act", lambda g, o=sqb[:, j * C1:(j + 1) * C1],
                           a=dst[:, j * C1:(j + 1) * C1]: g.activation(out=o, in_=a, func=AF.Square),
                           reads=[dstk + str(j)], writes=["sqb"])
                else:
                    P.emit(eng, lambda g, o=sqb[:, j * C1:(j + 1) * C1],
                           a=dst[:, j * C1:(j + 1) * C1]: g.tensor_mul(o, a, a),
                           reads=[dstk + str(j)], writes=["sqb"])

        def p1_chunk(c):
            if c + 1 < NC1:
                p1_xsq(c + 1)
            p1_qk_prod(c, wqb, "wq", q_c, "q_c",
                       ukf=(lambda dc: f"u0d{dc}") if c == 0 else None)
            p1_sq(q_c, "q_c", "dve")
            if c + 1 < NC1:
                p1_ln_a(c + 1)
                p1_ln_b(c + 1)
            p1_qk_prod(c, wkb, "wk", k_c, "k_c")
            mm_group([(obkb[:, 16 * j:16 * (j + 1)], sqb[:, j * C1:(j + 1) * C1],
                       PA[0:16, 0:512]) for j in range(8)],
                     ["PA0"], ["sqb", "obk"])
            P.emit("act", lambda s, o=rnq[:, c * C1:(c + 1) * C1]: s.copy(out=o, in_=PA[0:16, 0:512]),
                   reads=["PA0"], writes=["rows_nq"])
            p1_sq(k_c, "k_c", "dve")
            projrot = [("PC0", PC[:, 0:512]), ("PC1", PC[:, 512:1024]), ("PA0", PA[:, 0:512])]
            for p in range(8):
                pk, po = projrot[p % 3]
                mm_group([(rf2b[:], q_c[:, p * C1:(p + 1) * C1], po)], [pk], [f"q_c{p}", "rf2"])
                P.emit("act", lambda s, o=qt[:, c * 4096 + p * 512: c * 4096 + (p + 1) * 512], i=po:
                       s.activation(out=o, in_=i, func=AF.Exp, scale=cst[:, 2:3]),
                       reads=[pk, "cst"], writes=[f"qt{c}"])
            for s4 in range(4):
                nkk = "PA1" if s4 % 2 == 0 else "PD0"
                nkp = (PA[:, 512 + s4 * 16:512 + (s4 + 1) * 16] if s4 % 2 == 0
                       else PD[:, s4 * 16:(s4 + 1) * 16])
                mm_group([(sqb[:, j * C1 + s4 * 128: j * C1 + (s4 + 1) * 128],
                           obkb[:, 16 * j:16 * (j + 1)],
                           nkp) for j in range(8)],
                         [nkk], ["sqb", "obk"])
                P.emit("act", lambda s, o=nrmk[:, s4 * 16:(s4 + 1) * 16],
                       i=nkp:
                       s.activation(out=o, in_=i, func=AF.Ln),
                       reads=[nkk], writes=[f"nk{s4}"])
                P.emit("act", lambda s, o=nrmk2[:, s4 * 16:(s4 + 1) * 16],
                       i=nrmk[:, s4 * 16:(s4 + 1) * 16]:
                       s.activation(out=o, in_=i, func=AF.Exp, scale=cst[:, 4:5]),
                       reads=[f"nk{s4}", "cst"], writes=[f"nk2_{s4}"])
            sl = slice(c * C1, (c + 1) * C1)
            P.emit("act", lambda s, sl=sl: s.activation(out=rE[:, sl], in_=rnq[:, sl], func=AF.Ln),
                   reads=["rows_nq"], writes=["rows_E"])
            P.emit("act", lambda s, sl=sl: s.activation(out=rnq[:, sl], in_=rE[:, sl], func=AF.Exp,
                                                        scale=cst[0:16, 4:5]),
                   reads=["rows_E", "cst"], writes=["rows_nq"])
            P.emit("act", lambda s, sl=sl: s.activation(out=rE[:, sl], in_=rnq[:, sl], func=AF.Exp,
                                                        bias=cst[0:16, 3:4]),
                   reads=["rows_nq", "cst"], writes=["rows_E"])
            for s4 in range(4):
                PX = PC if s4 % 2 == 0 else PB
                pkx = ["PC0", "PC1"] if s4 % 2 == 0 else ["PB0", "PB1"]
                mm_group([(k_c[:, p * C1 + s4 * 128: p * C1 + (s4 + 1) * 128],
                           rf2b[:], PX[:, p * 128:(p + 1) * 128]) for p in range(8)],
                         pkx,
                         [f"k_c{p}" for p in range(8)] + ["rf2"],
                         starts=[True] * 8, stops=[True] * 8)
                kb = kpre[s4 % 2]
                P.emit("dve", lambda v, o=kb, s4=s4, PX=PX: v.scalar_tensor_tensor(
                    out=o.ap().rearrange("p (a b f) -> p a b f", a=8, b=2),
                    in0=PX.ap().rearrange("p (a b f) -> p a b f", a=8, b=2),
                    scalar=cst[:, 2:3],
                    in1=nrmk2[:, s4 * 16:(s4 + 1) * 16].rearrange("p (a b) -> p a b", a=8)[:, :, :, None]
                        .broadcast_to([128, 8, 2, 64]),
                    op0=OP.mult, op1=OP.subtract),
                    reads=pkx + [f"nk2_{s4}", "cst"], writes=[f"kpre{s4 % 2}"])
                P.emit("act", lambda s, o=kp_c[:, s4 * 1024:(s4 + 1) * 1024], i=kb[:]:
                       s.activation(out=o, in_=i, func=AF.Exp),
                       reads=[f"kpre{s4 % 2}"], writes=[f"kp{s4}"])
            ub, uk = ubuf[c % 2], f"u{c % 2}"
            for s4 in range(4):
                PX = PC if s4 % 2 == 0 else PB
                pkx = ["PC0", "PC1"] if s4 % 2 == 0 else ["PB0", "PB1"]
                prs, sts, sps = [], [], []
                for dc in range(8):
                    for r in range(2):
                        prs.append((ub[:, dc * C1 + s4 * 128: dc * C1 + (s4 + 1) * 128],
                                    wvb[:, dc * 1024 + r * 512: dc * 1024 + (r + 1) * 512],
                                    PX[:, r * 512:(r + 1) * 512]))
                        sts.append(dc == 0)
                        sps.append(dc == 7)
                mm_group(prs, pkx, [uk, "wv"], starts=sts, stops=sps)
                P.emit("act", lambda s, o=v_c[:, s4 * 1024:(s4 + 1) * 1024], i=PX[:]: s.copy(out=o, in_=i),
                       reads=pkx, writes=[f"v{s4}"])
            kvprs, kvst, kvsp = [], [], []
            sprs, sst, ssp = [], [], []
            for p in range(8):
                for s4 in range(4):
                    kvprs.append((kp_c[:, s4 * 1024 + p * 128: s4 * 1024 + (p + 1) * 128],
                                  v_c[:, s4 * 1024 + p * 128: s4 * 1024 + (p + 1) * 128],
                                  PD[:, p * 128:(p + 1) * 128]))
                    kvst.append(s4 == 0)
                    kvsp.append(s4 == 3)
                    sprs.append((kp_c[:, s4 * 1024 + p * 128: s4 * 1024 + (p + 1) * 128],
                                 onec[:, 0:1], PA[:, 512 + p:512 + p + 1]))
                    sst.append(s4 == 0)
                    ssp.append(s4 == 3)
            mm_group(kvprs, ["PD0", "PD1"],
                     [f"kp{s}" for s in range(4)] + [f"v{s}" for s in range(4)],
                     starts=kvst, stops=kvsp)
            P.emit("dve", lambda v: v.tensor_add(kvs[:], kvs[:], PD[:]),
                   reads=["PD0", "PD1", "KVs"], writes=["KVs"])
            mm_group(sprs, ["PA1"],
                     [f"kp{s}" for s in range(4)] + ["onec"],
                     starts=sst, stops=ssp)
            P.emit("dve", lambda v: v.tensor_add(ssb[:], ssb[:], PA[:, 512:520]),
                   reads=["PA1", "ssb"], writes=["ssb"])

        dma("pool", xin[0].ap().rearrange("p (dc t) -> p dc t", dc=8)[:, 0:4],
            xT_r[:, 0:4, 0:C1], "xin0h0")
        dma("pool", xin[0].ap().rearrange("p (dc t) -> p dc t", dc=8)[:, 4:8],
            xT_r[:, 4:8, 0:C1], "xin0h1")
        dma("pool", wqb.ap().rearrange("p (dc o) -> p dc o", dc=8)[:, :, 0:512],
            wqkvT_r[:, :, 0:512], "wqA")
        dma("pool", wqb.ap().rearrange("p (dc o) -> p dc o", dc=8)[:, :, 512:1024],
            wqkvT_r[:, :, 512:1024], "wqB")
        dma("pool", rf2b[:], rf2, "rf2")
        dma("pool", obkb[:], obk, "obk")
        dma("pool", wkb.ap().rearrange("p (dc o) -> p dc o", dc=8)[:, :, 0:512],
            wqkvT_r[:, :, 1024:1536], "wkA")
        dma("pool", wkb.ap().rearrange("p (dc o) -> p dc o", dc=8)[:, :, 512:1024],
            wqkvT_r[:, :, 1536:2048], "wkB")
        p1_dma_x(1)
        dma("pool", expfb[:], expf_h, "expf")
        dma("pool", wvb.ap().rearrange("p (dc o) -> p dc o", dc=8), wqkvT_r[:, :, 2048:3072], "wv")
        ub0, xb0 = ubuf[0], xin[0]
        mt0, rt0 = Mtb[0], Rt[0]
        for h in range(2):
            mm_group([(ones[:], xb0[:, dc * C1:(dc + 1) * C1], PA[:, 0:512])
                      for dc in range(4 * h, 4 * h + 4)],
                     ["PA0"], [f"xin0h{h}", "ones"],
                     starts=[h == 0 and dc == 0 for dc in range(4)],
                     stops=[h == 1 and dc == 3 for dc in range(4)])
        for h in range(2):
            P.emit("dve", lambda g, h=h,
                   o=ub0.ap().rearrange("p (dc t) -> p dc t", dc=8)[:, 4 * h:4 * h + 4],
                   a=xb0.ap().rearrange("p (dc t) -> p dc t", dc=8)[:, 4 * h:4 * h + 4]:
                   g.tensor_mul(o, a, a),
                   reads=[f"xin0h{h}"], writes=[f"u0h{h}"])
        for h in range(2):
            mm_group([(ones[:], ub0[:, dc * C1:(dc + 1) * C1], PA[:, 512:1024])
                      for dc in range(4 * h, 4 * h + 4)],
                     ["PA1"], [f"u0h{h}", "ones"],
                     starts=[h == 0 and dc == 0 for dc in range(4)],
                     stops=[h == 1 and dc == 3 for dc in range(4)])
        P.emit("act", lambda s, o=mt0[:]: s.mul(o, PA[:, 0:512], cst[:, 0:1]),
               reads=["PA0", "cst"], writes=["Mt0"])
        P.emit("dve", lambda v, m=mt0[:]: v.tensor_mul(msq[:], m, m), reads=["Mt0"], writes=["msq"])
        P.emit("dve", lambda v: v.scalar_tensor_tensor(
            out=Vt[:], in0=PA[:, 512:1024], scalar=cst[:, 0:1], in1=msq[:],
            op0=OP.mult, op1=OP.subtract), reads=["PA1", "msq", "cst"], writes=["Vt"])
        P.emit("act", lambda s: s.activation(out=msq[:], in_=Vt[:], func=AF.Ln, bias=cst[:, 1:2]),
               reads=["Vt", "cst"], writes=["msq"])
        P.emit("act", lambda s, o=rt0[:]: s.activation(out=o, in_=msq[:], func=AF.Exp, scale=cst[:, 5:6]),
               reads=["msq", "cst"], writes=["Rt0"])
        for dc in range(8):
            P.emit("dve", lambda v, o=xb0[:, dc * C1:(dc + 1) * C1], m=mt0[:]:
                   v.tensor_sub(o, o, m),
                   reads=[f"xin0h{dc // 4}", "Mt0"],
                   writes=[f"x0d{dc}"] + (["xin0"] if dc == 7 else []))
        for dc in range(8):
            P.emit("dve", lambda v, o=ub0[:, dc * C1:(dc + 1) * C1],
                   a=xb0[:, dc * C1:(dc + 1) * C1], r=rt0[:]:
                   v.tensor_mul(o, a, r),
                   reads=[f"x0d{dc}", f"xin0h{dc // 4}", "Rt0"],
                   writes=[f"u0d{dc}"] + (["u0"] if dc == 7 else []))
        for c in range(NC1):
            if c + 2 < NC1:
                p1_dma_x(c + 2)
            p1_chunk(c)

        sc_v = scols.ap().rearrange("p (g c) -> p g c", c=18)
        P.emit("act", lambda s: s.copy(out=sc_v[0:64, :, 0:1], in_=ssb[0:64, :, None]),
               reads=["ssb", "scols"], writes=["scols"])
        P.emit("act", lambda s: s.copy(out=sc_v[64:128, :, 1:2], in_=ssb[64:128, :, None]),
               reads=["ssb", "scols"], writes=["scols"])

        for p in range(8):
            P.emit("act", lambda s, p=p: s.copy(out=kvb[0:64, p * 128:p * 128 + 64],
                                                in_=kvs[0:64, p * 128:p * 128 + 64]),
                   reads=["KVs", "KVb"], writes=["KVb"])
            P.emit("act", lambda s, p=p: s.copy(out=kvb[64:128, p * 128 + 64:(p + 1) * 128],
                                                in_=kvs[64:128, p * 128 + 64:(p + 1) * 128]),
                   reads=["KVs", "KVb"], writes=["KVb"])

        Vt2 = kvs[:, 0:512]
        ms2 = kvs[:, 512:1024]
        rR = kvs[0:16, 0:512]

        def denom(i, pk="PA0", split=False):
            t0 = i * 512
            rrb = rRb[i % 2]
            pw = PA[0:16, 0:512] if pk == "PA0" else PA[0:16, 512:1024]
            mm_group([(scols[:, 16 * p:16 * (p + 1)], qt[:, i * 4096 + p * 512: i * 4096 + (p + 1) * 512],
                       pw) for p in range(8)],
                     [pk], ["scols", f"qt{i}"])
            P.emit("dve", lambda v, t0=t0, pw=pw: v.tensor_add(rD[:], pw,
                                                               rE[:, t0:t0 + 512]),
                   reads=[pk, "rows_E"], writes=["r2D"])
            if split:
                for h in range(2):
                    P.emit("dve", lambda v, h=h: v.reciprocal(
                        out=rR[:, h * 256:(h + 1) * 256], in_=rD[:, h * 256:(h + 1) * 256]),
                        reads=["r2D"], writes=["r2R", "KVs"])
                    P.emit("act", lambda s, rrb=rrb, h=h: s.copy(
                        out=rrb[:, h * 256:(h + 1) * 256], in_=rR[:, h * 256:(h + 1) * 256]),
                        reads=["r2R"], writes=[f"r2Rb{i % 2}h{h}"])
            else:
                P.emit("dve", lambda v: v.reciprocal(out=rR, in_=rD[:]),
                       reads=["r2D"], writes=["r2R", "KVs"])
                P.emit("act", lambda s, rrb=rrb: s.copy(out=rrb[:], in_=rR),
                       reads=["r2R"], writes=[f"r2Rb{i % 2}h0", f"r2Rb{i % 2}h1"])

        es1.close()
        P.barrier(compute_only=True, skip=("pe",))

        denom(0, split=True)
        denom(1, pk="PA1", split=True)

        es2 = ExitStack()
        sb2 = lambda n, sh, dt: es2.enter_context(nc.sbuf_tensor(n, sh, dt))
        x2 = sb2("x2_sb", [128, 8 * T], F32)
        u2 = sb2("u2_sb", [128, 8 * T], BF16)
        wob = sb2("woutT_bf", [128, 8 * 1024], BF16)
        dma("pool", wob.ap().rearrange("p (dc o) -> p dc o", dc=8), woutT_r[:, :, :], "wout")

        es2a = ExitStack()
        sb2a = lambda n, sh, dt: es2a.enter_context(nc.sbuf_tensor(n, sh, dt))
        xin2 = [sb2a(f"xin2_{i}", [128, 8 * C2], F32) for i in range(2)]
        attn = [sb2a(f"attn{i}", [128, 8 * 512], BF16) for i in range(2)]
        xb2 = sb2a("xb2", [128, 8 * 512], BF16)
        Mt2b = sb2a("Mt2b", [128, 512], BF16)
        Rt2b = sb2a("Rt2b", [128, 512], BF16)
        Rsb = [sb2a(f"Rsb{i}", [128, 512], BF16) for i in range(2)]

        W1g = [qt[:, 0:4096], qt[:, 4096:8192]]
        W2g = [qt[:, 8192:12288], qt[:, 12288:16384]]

        XB2K = [f"xb2_{j}" for j in range(8)]

        def ln2_copy(i, j):
            P.emit("act", lambda g, i=i, j=j: g.copy(
                out=xb2[:, j * 512:(j + 1) * 512],
                in_=x2[:, j * T + i * 512: j * T + (i + 1) * 512]),
                reads=[f"x2p{i}_{j}"], writes=[f"xb2_{j}"])

        def ln2_rest(i):
            u2b_ap = u2.ap().rearrange("p (i j t) -> p i j t", i=4, j=8)[:, i]
            uk = "u2a" if i < 2 else "u2b"
            mm_group([(ones[:], xb2[:, j * 512:(j + 1) * 512], PD[:, 0:512]) for j in range(8)],
                     ["PD0"], XB2K + ["ones"])
            P.emit("dve", lambda g, o=u2b_ap: g.tensor_mul(
                o, xb2.ap().rearrange("p (j t) -> p j t", j=8),
                xb2.ap().rearrange("p (j t) -> p j t", j=8)),
                reads=XB2K, writes=[uk])
            mm_group([(ones[:], u2[:, i * 4096 + j * 512: i * 4096 + (j + 1) * 512],
                       PD[:, 512:1024]) for j in range(8)],
                     ["PD1"], [uk, "ones"])
            P.emit("act", lambda s: s.mul(Mt2b[:], PD[:, 0:512], cst[:, 0:1]),
                   reads=["PD0", "cst"], writes=["Mt2b"])
            P.emit("dve", lambda v: v.tensor_mul(ms2, Mt2b[:], Mt2b[:]), reads=["Mt2b"], writes=["ms2"])
            P.emit("dve", lambda v: v.scalar_tensor_tensor(
                out=Vt2, in0=PD[:, 512:1024], scalar=cst[:, 0:1], in1=ms2,
                op0=OP.mult, op1=OP.subtract),
                reads=["PD1", "ms2", "cst", "r2R", "r2Rb0", "r2Rb1"], writes=["Vt2"])
            P.emit("act", lambda s: s.activation(out=ms2, in_=Vt2, func=AF.Ln, bias=cst[:, 1:2]),
                   reads=["Vt2", "cst"], writes=["ms2"])
            P.emit("act", lambda s: s.activation(out=Rt2b[:], in_=ms2, func=AF.Exp, scale=cst[:, 5:6]),
                   reads=["ms2", "cst"], writes=["Rt2"])
            P.emit("dve", lambda v: v.tensor_sub(
                xb2.ap().rearrange("p (j t) -> p j t", j=8),
                xb2.ap().rearrange("p (j t) -> p j t", j=8),
                Mt2b[:, None, :].broadcast_to([128, 8, 512])),
                reads=["Mt2b"] + XB2K, writes=XB2K)
            P.emit("dve", lambda v, o=u2b_ap: v.tensor_mul(
                o, xb2.ap().rearrange("p (j t) -> p j t", j=8),
                Rt2b[:, None, :].broadcast_to([128, 8, 512])),
                reads=["Rt2"] + XB2K, writes=[uk])

        def wout_j(i, j):
            ab = i % 2
            pk = "PC0" if j % 2 == 0 else "PC1"
            po = PC[:, 0:512] if j % 2 == 0 else PC[:, 512:1024]
            if j % 2 == 1 and j % 4 == 3:
                pass
            pt = PC if j % 2 == 0 else PD
            pks = ["PC0", "PC1"] if j % 2 == 0 else ["PD0", "PD1"]
            bank = (j % 4)
            src = [PC[:, 0:512], PC[:, 512:1024], PD[:, 0:512], PD[:, 512:1024]][bank]
            key = ["PC0", "PC1", "PD0", "PD1"][bank]
            mm_group([(wob[:, dc * 1024 + j * 128: dc * 1024 + (j + 1) * 128],
                       attn[ab][:, dc * 512:(dc + 1) * 512], src) for dc in range(8)],
                     [key], [f"at{ab}_{dc}" for dc in range(8)] + ["wout"])
            for cc in range(2):
                cg = 2 * i + cc
                P.emit("dve", lambda v, j=j, cg=cg, cc=cc, i_=src, xi=xin2[cg % 2]:
                       v.scalar_tensor_tensor(
                           out=x2[:, j * T + cg * C2: j * T + (cg + 1) * C2],
                           in0=i_[:, cc * C2:(cc + 1) * C2],
                           scalar=bout[:, j:j + 1], in1=xi[:, j * C2:(j + 1) * C2],
                           op0=OP.add, op1=OP.add),
                       reads=[key, "bout", f"xin2_{cg % 2}"], writes=[f"x2p{i}_{j}"])
            ln2_copy(i, j)

        dma("sp", xin2[0].ap().rearrange("p (j t) -> p j t", j=8), xT_r[:, :, 0:C2], "xin2_0")
        dma("sp", xin2[1].ap().rearrange("p (j t) -> p j t", j=8), xT_r[:, :, C2:2 * C2], "xin2_1")

        for i in range(NC2 // 2):
            t0 = i * 512
            ab = i % 2
            A = attn[ab]
            rrb = rRb[i % 2]
            for p in range(8):
                pkR = "PA0" if p % 2 == 0 else "PA1"
                poR = PA[:, 0:512] if p % 2 == 0 else PA[:, 512:1024]
                if i < 2:
                    for h in range(2):
                        mm_group([(expfb[:, p * 128:(p + 1) * 128],
                                   rrb[:, h * 256:(h + 1) * 256],
                                   poR[:, h * 256:(h + 1) * 256])],
                                 [pkR], [f"r2Rb{i % 2}h{h}", "expf"])
                else:
                    mm_group([(expfb[:, p * 128:(p + 1) * 128], rrb[:], poR)],
                             [pkR], [f"r2Rb{i % 2}h0", f"r2Rb{i % 2}h1", "expf"])
                rb = Rsb[p % 2]
                P.emit("act", lambda s, o=rb[:], i_=poR: s.copy(out=o, in_=i_),
                       reads=[pkR], writes=[f"Rsb{p % 2}"])
                pkN = "PB0" if p % 2 == 0 else "PB1"
                po = PB[:, 0:512] if p % 2 == 0 else PB[:, 512:1024]
                mm_group([(kvb[:, p * 128:(p + 1) * 128],
                           qt[:, i * 4096 + p * 512: i * 4096 + (p + 1) * 512], po)],
                         [pkN], ["KVb", f"qt{i}"])
                P.emit("dve", lambda v, o=A[:, p * 512:(p + 1) * 512], i_=po, r=rb[:]:
                       v.tensor_mul(o, i_, r),
                       reads=[pkN, f"Rsb{p % 2}"], writes=[f"at{ab}_{p}"])

            if i + 2 < NC2 // 2:
                denom(i + 2, pk="PA0" if i % 2 == 0 else "PA1")

            if i > 0:
                for j in range(8):
                    wout_j(i - 1, j)
                ln2_rest(i - 1)

            if i == 0:
                P.emit("pool", lambda e: e.dma_start(out=W1g[0].rearrange("p (dc i) -> p dc i", dc=8),
                                                     in_=w1_r[:, :, 0:GI]),
                       writes=["W1g0", "qt0"], dma=True)
            elif i == 1:
                P.emit("pool", lambda e: e.dma_start(out=W1g[1].rearrange("p (dc i) -> p dc i", dc=8),
                                                     in_=w1_r[:, :, GI:2 * GI]),
                       writes=["W1g1", "qt1"], dma=True)
            elif i == 2:
                P.emit("pool", lambda e: e.dma_start(out=W2g[0].rearrange("p (ic o) -> p ic o", ic=4),
                                                     in_=w2_r[:, 0:4, :]),
                       writes=["W2g0", "qt2"], dma=True)
            else:
                P.emit("pool", lambda e: e.dma_start(out=W2g[1].rearrange("p (ic o) -> p ic o", ic=4),
                                                     in_=w2_r[:, 4:8, :]),
                       writes=["W2g1", "qt3"], dma=True)

            if i >= 1:
                dma("sp", xin2[0].ap().rearrange("p (j t) -> p j t", j=8),
                    xT_r[:, :, (2 * i) * C2:(2 * i + 1) * C2], "xin2_0")
                dma("sp", xin2[1].ap().rearrange("p (j t) -> p j t", j=8),
                    xT_r[:, :, (2 * i + 1) * C2:(2 * i + 2) * C2], "xin2_1")

        for j in range(8):
            wout_j(NC2 // 2 - 1, j)

        es2a.close()

        h_sb = wob

        es2b = ExitStack()
        sb2b = lambda n, sh, dt: es2b.enter_context(nc.sbuf_tensor(n, sh, dt))
        ostg = [sb2b(f"ostg{i}", [128, TH], F32) for i in range(3)]

        def load_w1(g):
            dma("pool", W1g[g % 2].rearrange("p (dc i) -> p dc i", dc=8),
                w1_r[:, :, g * GI:(g + 1) * GI], f"W1g{g % 2}")

        def load_w2(g):
            dma("pool", W2g[g % 2].rearrange("p (ic o) -> p ic o", ic=4),
                w2_r[:, g * 4:(g + 1) * 4, :], f"W2g{g % 2}")

        ns = 0
        for gp in range(NG // 2):
            for th in range(NTH):
                for gg in range(2):
                    g = 2 * gp + gg
                    buf = g % 2
                    for ic in range(4):
                        prs, sts, sps = [], [], []
                        for dc in range(8):
                            for r in range(2):
                                pr = 2 * th + r
                                prs.append((W1g[buf][:, dc * 512 + ic * 128: dc * 512 + (ic + 1) * 128],
                                            u2[:, pr * 4096 + dc * 512: pr * 4096 + (dc + 1) * 512],
                                            PA[:, r * 512:(r + 1) * 512] if ic % 2 == 0
                                            else PB[:, r * 512:(r + 1) * 512]))
                                sts.append(dc == 0)
                                sps.append(dc == 7)
                        pk = ["PA0", "PA1"] if ic % 2 == 0 else ["PB0", "PB1"]
                        mm_group(prs, pk, ["u2a" if th == 0 else "u2b", f"W1g{buf}"],
                                 starts=sts, stops=sps)
                        src = PA if ic % 2 == 0 else PB
                        P.emit("act", lambda s, o=h_sb[:, gg * 4096 + ic * 1024: gg * 4096 + (ic + 1) * 1024],
                               i=src[:], b=b1s[:, g * 4 + ic:g * 4 + ic + 1]:
                               s.activation(out=o, in_=i, func=AF.Relu, bias=b),
                               reads=pk + ["b1"], writes=[f"h{gg}_{ic}", "wout"])
                    if gp == 0 and th == 0 and gg == 0:
                        ln2_rest(NC2 // 2 - 1)
                    if th == NTH - 1 and g + 2 < NG:
                        load_w1(g + 2)
                for j in range(8):
                    pt, pks = (PC, ["PC0", "PC1"]) if j % 2 == 0 else (PD, ["PD0", "PD1"])
                    prs, sts, sps = [], [], []
                    for icg in range(8):
                        gg2, ic = icg // 4, icg % 4
                        bufw = (2 * gp + gg2) % 2
                        for r in range(2):
                            prs.append((W2g[bufw][:, ic * 1024 + j * 128: ic * 1024 + (j + 1) * 128],
                                        h_sb[:, gg2 * 4096 + ic * 1024 + r * 512:
                                             gg2 * 4096 + ic * 1024 + (r + 1) * 512],
                                        pt[:, r * 512:(r + 1) * 512]))
                            sts.append(icg == 0)
                            sps.append(icg == 7)
                    hkeys = [f"h{gg2}_{ic}" for gg2 in range(2) for ic in range(4)]
                    if j == 0:
                        mm_group(prs[:14], pks, hkeys[:7] + ["W2g0", "W2g1"],
                                 starts=sts[:14], stops=[False] * 14)
                        mm_group(prs[14:], pks, [hkeys[7]],
                                 starts=[False, False], stops=sps[14:])
                    else:
                        mm_group(prs, pks, hkeys + ["W2g0", "W2g1"],
                                 starts=sts, stops=sps)
                    x2k = [f"x2p{2 * th}_{j}", f"x2p{2 * th + 1}_{j}"]
                    if gp < NG // 2 - 1:
                        P.emit("dve", lambda v, j=j, th=th, i=pt[:]: v.tensor_add(
                            x2[:, j * T + th * TH: j * T + (th + 1) * TH],
                            x2[:, j * T + th * TH: j * T + (th + 1) * TH], i),
                            reads=pks + x2k, writes=x2k)
                    else:
                        ob = ostg[ns % 3]
                        P.emit("dve", lambda v, j=j, th=th, i=pt[:], o=ob[:]: v.scalar_tensor_tensor(
                            out=o, in0=i, scalar=b2s[:, j:j + 1],
                            in1=x2[:, j * T + th * TH: j * T + (th + 1) * TH],
                            op0=OP.add, op1=OP.add),
                            reads=pks + x2k + ["b2"], writes=[f"ostg{ns % 3}"])
                        dma("sp", outT_r[:, j, th * TH:(th + 1) * TH], ob[:],
                            f"od{ns}", rkeys=[f"ostg{ns % 3}"])
                        ns += 1
                if th == NTH - 1 and gp < NG // 2 - 1:
                    load_w2(2 * gp + 2)
                    load_w2(2 * gp + 3)

        es2b.close()
        es2.close()
        P.final_wait("sp", "sp")

        with nc.Block() as block:
            def runner(name):
                def run(eng):
                    for waits, fn, sem in P.ops[name]:
                        for (s, v) in waits:
                            eng.wait_ge(semh[s], v)
                        if fn is not None:
                            ins = fn(eng)
                            ins.then_inc(semh[sem], 1 if sem.startswith("c_") else 16)
                return run
            block.tensor(runner("pe"))
            block.scalar(runner("act"))
            block.vector(runner("dve"))
            block.gpsimd(runner("pool"))
            block.sync(runner("sp"))

    return nc


def kernel(L0_attn_x, L0_attn_w_qkv, L0_attn_w_out, L0_attn_b_out,
           L0_attn_random_features, L0_ffn_W1, L0_ffn_b1, L0_ffn_W2, L0_ffn_b2):
    x = np.asarray(L0_attn_x, dtype=np.float32)
    wqkvT = np.ascontiguousarray(np.asarray(L0_attn_w_qkv, np.float32).T)
    woutT = np.ascontiguousarray(np.asarray(L0_attn_w_out, np.float32).T)
    w1 = np.ascontiguousarray(np.asarray(L0_ffn_W1, np.float32))
    w2 = np.ascontiguousarray(np.asarray(L0_ffn_W2, np.float32))
    rf = np.asarray(L0_attn_random_features, np.float32)
    b_out = np.asarray(L0_attn_b_out, np.float32)
    b1 = np.asarray(L0_ffn_b1, np.float32)
    b2 = np.asarray(L0_ffn_b2, np.float32)

    rf2 = np.zeros((128, 128), np.float32)
    rf2[0:64, 0:64] = rf.T
    rf2[64:128, 64:128] = rf.T
    obk = np.zeros((128, 128), np.float32)
    for p in range(8):
        obk[0:64, 16 * p + 2 * p] = 1.0
        obk[64:128, 16 * p + 2 * p + 1] = 1.0
    expf = np.zeros((16, 1024), np.float32)
    for p in range(8):
        expf[2 * p, p * 128:p * 128 + 64] = 1.0
        expf[2 * p + 1, p * 128 + 64:(p + 1) * 128] = 1.0
    bout_h = np.ascontiguousarray(b_out.reshape(8, 128).T)
    b1_h = np.ascontiguousarray(b1.reshape(32, 128).T)
    b2_h = np.ascontiguousarray(b2.reshape(8, 128).T)

    nc = build_nc()
    in_maps = []
    for c in range(NCORES):
        in_maps.append({
            "xT": np.ascontiguousarray(x[c].T),
            "wqkvT": wqkvT, "woutT": woutT, "w1": w1, "w2": w2,
            "rf2": rf2, "obk": obk, "expf": expf,
            "bout": bout_h, "b1": b1_h, "b2": b2_h,
        })
    res = run_bass_kernel_spmd(nc, in_maps, core_ids=list(range(NCORES)))
    out = np.stack([np.ascontiguousarray(r["outT"].T) for r in res.results], axis=0)
    return out


# version 73
# speedup vs baseline: 1.0073x; 1.0061x over previous
import math
import numpy as np
from contextlib import ExitStack

import concourse.bass as bass
import concourse.mybir as mybir
from concourse.bass_utils import run_bass_kernel_spmd

F32 = mybir.dt.float32
BF16 = mybir.dt.bfloat16
AF = mybir.ActivationFunctionType
OP = mybir.AluOpType

NCORES = 8
D = 1024
T = 2048
INTER = 4096

C1 = 512
NC1 = T // C1
C2 = 256
NC2 = T // C2
TH = 1024
NTH = T // TH
NG = 8
GI = INTER // NG

ENGINES = ("pe", "act", "dve", "pool", "sp")


class Plan:

    def __init__(self):
        self.ops = {e: [] for e in ENGINES}
        self.dma_slots = {"pool": [f"dp{i}" for i in range(8)],
                          "sp": [f"ds{i}" for i in range(14)]}
        self.dma_rr = {"pool": 0, "sp": 0}
        self.sems = (["c_pe", "c_act", "c_dve", "c_pool"]
                     + self.dma_slots["pool"] + self.dma_slots["sp"])
        self.cnt = {s: 0 for s in self.sems}
        self.water = {}
        self.reg = {}

    def _want(self, eng, waits, sem, val):
        if val <= 0:
            return
        if eng == "pe" and sem == "c_pe":
            return
        k = (eng, sem)
        if self.water.get(k, 0) >= val:
            return
        self.water[k] = val
        waits.append((sem, val))

    def emit(self, eng, fn, reads=(), writes=(), dma=False):
        waits = []
        if dma:
            slots = self.dma_slots[eng]
            sem = slots[self.dma_rr[eng] % len(slots)]
            self.dma_rr[eng] += 1
            self._want(eng, waits, sem, self.cnt[sem])
        else:
            sem = "c_" + eng
        for r in reads:
            info = self.reg.get(r)
            if info and info["w"] is not None:
                self._want(eng, waits, *info["w"])
        for w in writes:
            info = self.reg.get(w)
            if info:
                if info["w"] is not None:
                    self._want(eng, waits, *info["w"])
                for rd in info["rs"]:
                    self._want(eng, waits, *rd)
        self.cnt[sem] += 16 if dma else 1
        val = self.cnt[sem]
        self.ops[eng].append((waits, fn, sem))
        for r in reads:
            self.reg.setdefault(r, {"w": None, "rs": []})["rs"].append((sem, val))
        for w in writes:
            self.reg[w] = {"w": (sem, val), "rs": []}

    def barrier(self, compute_only=False, skip=()):
        snap = dict(self.cnt)
        for e in ENGINES:
            if e in skip:
                continue
            waits = []
            for s, v in snap.items():
                if s == "c_" + e:
                    continue
                if compute_only and not s.startswith("c_"):
                    continue
                self._want(e, waits, s, v)
            if waits:
                self.ops[e].append((waits, None, None))

    def final_wait(self, eng, queue):
        waits = [(sem, self.cnt[sem]) for sem in self.dma_slots[queue] if self.cnt[sem] > 0]
        self.ops[eng].append((waits, None, None))


def build_nc():
    nc = bass.Bass("TRN2", target_bir_lowering=False)

    xT = nc.dram_tensor("xT", [D, T], F32, kind="ExternalInput").ap()
    wqkvT = nc.dram_tensor("wqkvT", [D, 3 * D], F32, kind="ExternalInput").ap()
    woutT = nc.dram_tensor("woutT", [D, D], F32, kind="ExternalInput").ap()
    w1 = nc.dram_tensor("w1", [D, INTER], F32, kind="ExternalInput").ap()
    w2 = nc.dram_tensor("w2", [INTER, D], F32, kind="ExternalInput").ap()
    rf2 = nc.dram_tensor("rf2", [128, 128], F32, kind="ExternalInput").ap()
    obk = nc.dram_tensor("obk", [128, 128], F32, kind="ExternalInput").ap()
    expf_h = nc.dram_tensor("expf", [16, 1024], F32, kind="ExternalInput").ap()
    bout_h = nc.dram_tensor("bout", [128, 8], F32, kind="ExternalInput").ap()
    b1_h = nc.dram_tensor("b1", [128, 32], F32, kind="ExternalInput").ap()
    b2_h = nc.dram_tensor("b2", [128, 8], F32, kind="ExternalInput").ap()
    outT = nc.dram_tensor("outT", [D, T], F32, kind="ExternalOutput").ap()

    xT_r = xT.rearrange("(dc p) t -> p dc t", p=128)
    wqkvT_r = wqkvT.rearrange("(dc p) o -> p dc o", p=128)
    woutT_r = woutT.rearrange("(dc p) o -> p dc o", p=128)
    w1_r = w1.rearrange("(dc p) i -> p dc i", p=128)
    w2_r = w2.rearrange("(ic p) o -> p ic o", p=128)
    outT_r = outT.rearrange("(j p) t -> p j t", p=128)

    P = Plan()
    es = ExitStack()
    with es:
        sbuf = lambda n, sh, dt: es.enter_context(nc.sbuf_tensor(n, sh, dt))
        psumt = lambda n, sh: es.enter_context(nc.psum_tensor(n, sh, F32))

        cst = sbuf("cst", [128, 8], F32)
        bout = sbuf("bout_sb", [128, 8], F32)
        b1s = sbuf("b1_sb", [128, 32], F32)
        b2s = sbuf("b2_sb", [128, 8], F32)
        rf2b = sbuf("rf2_bf", [128, 128], BF16)
        obkb = sbuf("obk_bf", [128, 128], BF16)
        ones = sbuf("ones_bf", [128, 128], BF16)
        onec = sbuf("onec_bf", [128, 8], BF16)
        expfb = sbuf("expf_bf", [16, 1024], BF16)
        rE = sbuf("rE", [16, 2048], BF16)
        kvs = sbuf("KV_sb", [128, 1024], F32)
        ssb = sbuf("s_sb", [128, 8], F32)
        scols = sbuf("s_cols", [128, 144], BF16)
        qt = sbuf("qt_sb", [128, 8 * T], BF16)
        kvb = sbuf("KV_bf", [128, 1024], BF16)
        rD = sbuf("rD", [16, 512], F32)
        rRb = [sbuf(f"rRb{i}", [16, 512], BF16) for i in range(2)]

        PA = psumt("PA", [128, 1024])
        PB = psumt("PB", [128, 1024])
        PC = psumt("PC", [128, 1024])
        PD = psumt("PD", [128, 1024])

        semh = {s: es.enter_context(nc.semaphore(s)) for s in P.sems}

        def mset(buf_ap, val, key):
            P.emit("dve", lambda v, a=buf_ap, x=val: v.memset(a, x), writes=[key])

        mset(cst[:, 0:1], 1.0 / 1024.0, "cst")
        for i, v in enumerate([1e-5, 0.125, math.log(1e-6), 0.5, -0.5]):
            P.emit("dve", lambda vv, a=cst[:, i + 1:i + 2], x=v: vv.memset(a, x), writes=["cst"])
        mset(ones[:], 1.0, "ones")
        mset(onec[:], 1.0, "onec")
        mset(kvs[:], 0.0, "KVs")
        mset(ssb[:], 0.0, "ssb")
        mset(scols[:], 0.0, "scols")
        mset(kvb[:], 0.0, "KVb")

        def dma(eng, out_ap, in_ap, wkey, rkeys=()):
            wks = [wkey] if isinstance(wkey, str) else list(wkey)
            P.emit(eng, lambda e, o=out_ap, i=in_ap: e.dma_start(out=o, in_=i),
                   reads=rkeys, writes=wks, dma=True)

        dma("sp", bout[:], bout_h, "bout")
        dma("sp", b1s[:], b1_h, "b1")
        dma("sp", b2s[:], b2_h, "b2")


        def mm_group(pairs, wkeys, rkeys, starts=None, stops=None):
            n = len(pairs)
            if starts is None:
                starts = [i == 0 for i in range(n)]
            if stops is None:
                stops = [i == n - 1 for i in range(n)]

            def fn(t, pairs=pairs, starts=starts, stops=stops):
                ins = None
                for (lh, rh, oa), st, sp in zip(pairs, starts, stops):
                    ins = t.matmul(oa, lh, rh, start=st, stop=sp)
                return ins
            P.emit("pe", fn, reads=rkeys, writes=wkeys)

        es1 = ExitStack()
        sb1 = lambda n, sh, dt: es1.enter_context(nc.sbuf_tensor(n, sh, dt))
        rnq = sb1("rnq", [16, 2048], BF16)
        wqb = sb1("wq_bf", [128, 8 * 1024], BF16)
        wkb = sb1("wk_bf", [128, 8 * 1024], BF16)
        wvb = sb1("wv_bf", [128, 8 * 1024], BF16)
        xin = [sb1(f"xin{i}", [128, 8 * C1], BF16) for i in range(2)]
        ubuf = [sb1(f"u{i}", [128, 8 * C1], BF16) for i in range(2)]
        sqb = sb1("sqb", [128, 8 * C1], BF16)
        q_c = sb1("q_c", [128, 8 * C1], BF16)
        k_c = sb1("k_c", [128, 8 * C1], BF16)
        kpre = [sb1(f"kpre{i}", [128, 1024], F32) for i in range(2)]
        kp_c = sb1("kp_c", [128, 4 * 1024], BF16)
        v_c = sb1("v_c", [128, 4 * 1024], BF16)
        nrmk = sb1("nrmk", [128, 4 * 16], F32)
        nrmk2 = sb1("nrmk2", [128, 4 * 16], F32)
        Mtb = [sb1(f"Mtb{i}", [128, C1], BF16) for i in range(2)]
        Rt = [sb1(f"Rt{i}", [128, C1], BF16) for i in range(2)]
        Vt = sb1("Vt", [128, C1], F32)
        msq = sb1("msq", [128, C1], F32)

        def p1_dma_x(c):
            xb = xin[c % 2]
            wk = ([f"xin{c % 2}"] + (["xin0h0", "xin0h1"] if c == 2 else []))
            dma("pool", xb.ap().rearrange("p (dc t) -> p dc t", dc=8),
                xT_r[:, :, c * C1:(c + 1) * C1], wk)

        def p1_xsq(c):
            xb, xk = xin[c % 2], f"xin{c % 2}"
            ub, uk = ubuf[c % 2], f"u{c % 2}"
            P.emit("dve", lambda g, o=ub[:], a=xb[:]: g.tensor_mul(o, a, a),
                   reads=[xk], writes=[uk])

        def p1_ln_a(c):
            xb, xk = xin[c % 2], f"xin{c % 2}"
            ub, uk = ubuf[c % 2], f"u{c % 2}"
            mt = Mtb[c % 2]
            mtk = f"Mt{c % 2}"
            mm_group([(ones[:], xb[:, dc * C1:(dc + 1) * C1], PA[:, 0:512]) for dc in range(8)],
                     ["PA0"], [xk, "ones"])
            mm_group([(ones[:], ub[:, dc * C1:(dc + 1) * C1], PA[:, 512:1024]) for dc in range(8)],
                     ["PA1"], [uk, "ones"])
            P.emit("act", lambda s, o=mt[:]: s.mul(o, PA[:, 0:512], cst[:, 0:1]),
                   reads=["PA0", "cst"], writes=[mtk])

        def p1_ln_b(c):
            xb, xk = xin[c % 2], f"xin{c % 2}"
            ub, uk = ubuf[c % 2], f"u{c % 2}"
            mt, rt = Mtb[c % 2], Rt[c % 2]
            mtk, rtk = f"Mt{c % 2}", f"Rt{c % 2}"
            P.emit("dve", lambda v, m=mt[:]: v.tensor_mul(msq[:], m, m), reads=[mtk], writes=["msq"])
            P.emit("dve", lambda v: v.scalar_tensor_tensor(
                out=Vt[:], in0=PA[:, 512:1024], scalar=cst[:, 0:1], in1=msq[:],
                op0=OP.mult, op1=OP.subtract), reads=["PA1", "msq", "cst"], writes=["Vt"])
            P.emit("act", lambda s: s.activation(out=msq[:], in_=Vt[:], func=AF.Ln, bias=cst[:, 1:2]),
                   reads=["Vt", "cst"], writes=["msq"])
            P.emit("act", lambda s, o=rt[:]: s.activation(out=o, in_=msq[:], func=AF.Exp, scale=cst[:, 5:6]),
                   reads=["msq", "cst"], writes=[rtk])
            P.emit("dve", lambda v, xb=xb, m=mt: v.tensor_sub(
                xb.ap().rearrange("p (dc t) -> p dc t", dc=8),
                xb.ap().rearrange("p (dc t) -> p dc t", dc=8),
                m[:, None, :].broadcast_to([128, 8, C1])),
                reads=[xk, mtk], writes=[xk])
            P.emit("dve", lambda v, xb=xb, ub=ub, r=rt: v.tensor_mul(
                ub.ap().rearrange("p (dc t) -> p dc t", dc=8),
                xb.ap().rearrange("p (dc t) -> p dc t", dc=8),
                r[:, None, :].broadcast_to([128, 8, C1])),
                reads=[xk, rtk], writes=[uk])

        def p1_qk_prod(c, wgt, wk_, dst, dstk, ukf=None):
            ub, uk = ubuf[c % 2], f"u{c % 2}"
            for j in range(8):
                pk = "PB0" if j % 2 == 0 else "PB1"
                po = PB[:, 0:512] if j % 2 == 0 else PB[:, 512:1024]
                wk = wk_ + ("A" if j < 4 else "B")
                if ukf is None:
                    mm_group([(wgt[:, dc * 1024 + j * 128: dc * 1024 + (j + 1) * 128],
                               ub[:, dc * C1:(dc + 1) * C1], po) for dc in range(8)],
                             [pk], [uk, wk])
                else:
                    for dc in range(8):
                        mm_group([(wgt[:, dc * 1024 + j * 128: dc * 1024 + (j + 1) * 128],
                                   ub[:, dc * C1:(dc + 1) * C1], po)],
                                 [pk], [ukf(dc), wk],
                                 starts=[dc == 0], stops=[dc == 7])
                P.emit("act", lambda s, o=dst[:, j * C1:(j + 1) * C1], i=po: s.copy(out=o, in_=i),
                       reads=[pk], writes=[dstk + str(j)])

        def p1_sq(dst, dstk, eng):
            for j in range(8):
                if eng == "act":
                    P.emit("act", lambda g, o=sqb[:, j * C1:(j + 1) * C1],
                           a=dst[:, j * C1:(j + 1) * C1]: g.activation(out=o, in_=a, func=AF.Square),
                           reads=[dstk + str(j)], writes=["sqb"])
                else:
                    P.emit(eng, lambda g, o=sqb[:, j * C1:(j + 1) * C1],
                           a=dst[:, j * C1:(j + 1) * C1]: g.tensor_mul(o, a, a),
                           reads=[dstk + str(j)], writes=["sqb"])

        def p1_chunk(c):
            if c + 1 < NC1:
                p1_xsq(c + 1)
            p1_qk_prod(c, wqb, "wq", q_c, "q_c",
                       ukf=(lambda dc: f"u0d{dc}") if c == 0 else None)
            p1_sq(q_c, "q_c", "dve")
            if c + 1 < NC1:
                p1_ln_a(c + 1)
                p1_ln_b(c + 1)
            p1_qk_prod(c, wkb, "wk", k_c, "k_c")
            mm_group([(obkb[:, 16 * j:16 * (j + 1)], sqb[:, j * C1:(j + 1) * C1],
                       PA[0:16, 0:512]) for j in range(8)],
                     ["PA0"], ["sqb", "obk"])
            P.emit("act", lambda s, o=rnq[:, c * C1:(c + 1) * C1]: s.copy(out=o, in_=PA[0:16, 0:512]),
                   reads=["PA0"], writes=["rows_nq"])
            p1_sq(k_c, "k_c", "dve")
            projrot = [("PC0", PC[:, 0:512]), ("PC1", PC[:, 512:1024]), ("PA0", PA[:, 0:512])]
            for p in range(8):
                pk, po = projrot[p % 3]
                mm_group([(rf2b[:], q_c[:, p * C1:(p + 1) * C1], po)], [pk], [f"q_c{p}", "rf2"])
                P.emit("act", lambda s, o=qt[:, c * 4096 + p * 512: c * 4096 + (p + 1) * 512], i=po:
                       s.activation(out=o, in_=i, func=AF.Exp, scale=cst[:, 2:3]),
                       reads=[pk, "cst"], writes=[f"qt{c}"])
            for s4 in range(4):
                nkk = "PA1" if s4 % 2 == 0 else "PD0"
                nkp = (PA[:, 512 + s4 * 16:512 + (s4 + 1) * 16] if s4 % 2 == 0
                       else PD[:, s4 * 16:(s4 + 1) * 16])
                mm_group([(sqb[:, j * C1 + s4 * 128: j * C1 + (s4 + 1) * 128],
                           obkb[:, 16 * j:16 * (j + 1)],
                           nkp) for j in range(8)],
                         [nkk], ["sqb", "obk"])
                P.emit("act", lambda s, o=nrmk[:, s4 * 16:(s4 + 1) * 16],
                       i=nkp:
                       s.activation(out=o, in_=i, func=AF.Ln),
                       reads=[nkk], writes=[f"nk{s4}"])
                P.emit("act", lambda s, o=nrmk2[:, s4 * 16:(s4 + 1) * 16],
                       i=nrmk[:, s4 * 16:(s4 + 1) * 16]:
                       s.activation(out=o, in_=i, func=AF.Exp, scale=cst[:, 4:5]),
                       reads=[f"nk{s4}", "cst"], writes=[f"nk2_{s4}"])
            sl = slice(c * C1, (c + 1) * C1)
            P.emit("act", lambda s, sl=sl: s.activation(out=rE[:, sl], in_=rnq[:, sl], func=AF.Ln),
                   reads=["rows_nq"], writes=["rows_E"])
            P.emit("act", lambda s, sl=sl: s.activation(out=rnq[:, sl], in_=rE[:, sl], func=AF.Exp,
                                                        scale=cst[0:16, 4:5]),
                   reads=["rows_E", "cst"], writes=["rows_nq"])
            P.emit("act", lambda s, sl=sl: s.activation(out=rE[:, sl], in_=rnq[:, sl], func=AF.Exp,
                                                        bias=cst[0:16, 3:4]),
                   reads=["rows_nq", "cst"], writes=["rows_E"])
            for s4 in range(4):
                PX = PC if s4 % 2 == 0 else PB
                pkx = ["PC0", "PC1"] if s4 % 2 == 0 else ["PB0", "PB1"]
                mm_group([(k_c[:, p * C1 + s4 * 128: p * C1 + (s4 + 1) * 128],
                           rf2b[:], PX[:, p * 128:(p + 1) * 128]) for p in range(8)],
                         pkx,
                         [f"k_c{p}" for p in range(8)] + ["rf2"],
                         starts=[True] * 8, stops=[True] * 8)
                kb = kpre[s4 % 2]
                P.emit("dve", lambda v, o=kb, s4=s4, PX=PX: v.scalar_tensor_tensor(
                    out=o.ap().rearrange("p (a b f) -> p a b f", a=8, b=2),
                    in0=PX.ap().rearrange("p (a b f) -> p a b f", a=8, b=2),
                    scalar=cst[:, 2:3],
                    in1=nrmk2[:, s4 * 16:(s4 + 1) * 16].rearrange("p (a b) -> p a b", a=8)[:, :, :, None]
                        .broadcast_to([128, 8, 2, 64]),
                    op0=OP.mult, op1=OP.subtract),
                    reads=pkx + [f"nk2_{s4}", "cst"], writes=[f"kpre{s4 % 2}"])
                P.emit("act", lambda s, o=kp_c[:, s4 * 1024:(s4 + 1) * 1024], i=kb[:]:
                       s.activation(out=o, in_=i, func=AF.Exp),
                       reads=[f"kpre{s4 % 2}"], writes=[f"kp{s4}"])
            ub, uk = ubuf[c % 2], f"u{c % 2}"
            for s4 in range(4):
                PX = PC if s4 % 2 == 0 else PB
                pkx = ["PC0", "PC1"] if s4 % 2 == 0 else ["PB0", "PB1"]
                prs, sts, sps = [], [], []
                for dc in range(8):
                    for r in range(2):
                        prs.append((ub[:, dc * C1 + s4 * 128: dc * C1 + (s4 + 1) * 128],
                                    wvb[:, dc * 1024 + r * 512: dc * 1024 + (r + 1) * 512],
                                    PX[:, r * 512:(r + 1) * 512]))
                        sts.append(dc == 0)
                        sps.append(dc == 7)
                mm_group(prs, pkx, [uk, "wv"], starts=sts, stops=sps)
                P.emit("act", lambda s, o=v_c[:, s4 * 1024:(s4 + 1) * 1024], i=PX[:]: s.copy(out=o, in_=i),
                       reads=pkx, writes=[f"v{s4}"])
            kvprs, kvst, kvsp = [], [], []
            sprs, sst, ssp = [], [], []
            for p in range(8):
                for s4 in range(4):
                    kvprs.append((kp_c[:, s4 * 1024 + p * 128: s4 * 1024 + (p + 1) * 128],
                                  v_c[:, s4 * 1024 + p * 128: s4 * 1024 + (p + 1) * 128],
                                  PD[:, p * 128:(p + 1) * 128]))
                    kvst.append(s4 == 0)
                    kvsp.append(s4 == 3)
                    sprs.append((kp_c[:, s4 * 1024 + p * 128: s4 * 1024 + (p + 1) * 128],
                                 onec[:, 0:1], PA[:, 512 + p:512 + p + 1]))
                    sst.append(s4 == 0)
                    ssp.append(s4 == 3)
            mm_group(kvprs, ["PD0", "PD1"],
                     [f"kp{s}" for s in range(4)] + [f"v{s}" for s in range(4)],
                     starts=kvst, stops=kvsp)
            P.emit("dve", lambda v: v.tensor_add(kvs[:], kvs[:], PD[:]),
                   reads=["PD0", "PD1", "KVs"], writes=["KVs"])
            mm_group(sprs, ["PA1"],
                     [f"kp{s}" for s in range(4)] + ["onec"],
                     starts=sst, stops=ssp)
            P.emit("dve", lambda v: v.tensor_add(ssb[:], ssb[:], PA[:, 512:520]),
                   reads=["PA1", "ssb"], writes=["ssb"])

        dma("pool", xin[0].ap().rearrange("p (dc t) -> p dc t", dc=8)[:, 0:4],
            xT_r[:, 0:4, 0:C1], "xin0h0")
        dma("pool", xin[0].ap().rearrange("p (dc t) -> p dc t", dc=8)[:, 4:8],
            xT_r[:, 4:8, 0:C1], "xin0h1")
        dma("pool", wqb.ap().rearrange("p (dc o) -> p dc o", dc=8)[:, :, 0:512],
            wqkvT_r[:, :, 0:512], "wqA")
        dma("pool", wqb.ap().rearrange("p (dc o) -> p dc o", dc=8)[:, :, 512:1024],
            wqkvT_r[:, :, 512:1024], "wqB")
        dma("pool", rf2b[:], rf2, "rf2")
        dma("pool", obkb[:], obk, "obk")
        dma("pool", wkb.ap().rearrange("p (dc o) -> p dc o", dc=8)[:, :, 0:512],
            wqkvT_r[:, :, 1024:1536], "wkA")
        dma("pool", wkb.ap().rearrange("p (dc o) -> p dc o", dc=8)[:, :, 512:1024],
            wqkvT_r[:, :, 1536:2048], "wkB")
        p1_dma_x(1)
        dma("pool", expfb[:], expf_h, "expf")
        dma("pool", wvb.ap().rearrange("p (dc o) -> p dc o", dc=8), wqkvT_r[:, :, 2048:3072], "wv")
        ub0, xb0 = ubuf[0], xin[0]
        mt0, rt0 = Mtb[0], Rt[0]
        for h in range(2):
            mm_group([(ones[:], xb0[:, dc * C1:(dc + 1) * C1], PA[:, 0:512])
                      for dc in range(4 * h, 4 * h + 4)],
                     ["PA0"], [f"xin0h{h}", "ones"],
                     starts=[h == 0 and dc == 0 for dc in range(4)],
                     stops=[h == 1 and dc == 3 for dc in range(4)])
        for h in range(2):
            P.emit("dve", lambda g, h=h,
                   o=ub0.ap().rearrange("p (dc t) -> p dc t", dc=8)[:, 4 * h:4 * h + 4],
                   a=xb0.ap().rearrange("p (dc t) -> p dc t", dc=8)[:, 4 * h:4 * h + 4]:
                   g.tensor_mul(o, a, a),
                   reads=[f"xin0h{h}"], writes=[f"u0h{h}"])
        for h in range(2):
            mm_group([(ones[:], ub0[:, dc * C1:(dc + 1) * C1], PA[:, 512:1024])
                      for dc in range(4 * h, 4 * h + 4)],
                     ["PA1"], [f"u0h{h}", "ones"],
                     starts=[h == 0 and dc == 0 for dc in range(4)],
                     stops=[h == 1 and dc == 3 for dc in range(4)])
        P.emit("act", lambda s, o=mt0[:]: s.mul(o, PA[:, 0:512], cst[:, 0:1]),
               reads=["PA0", "cst"], writes=["Mt0"])
        P.emit("dve", lambda v, m=mt0[:]: v.tensor_mul(msq[:], m, m), reads=["Mt0"], writes=["msq"])
        P.emit("dve", lambda v: v.scalar_tensor_tensor(
            out=Vt[:], in0=PA[:, 512:1024], scalar=cst[:, 0:1], in1=msq[:],
            op0=OP.mult, op1=OP.subtract), reads=["PA1", "msq", "cst"], writes=["Vt"])
        P.emit("act", lambda s: s.activation(out=msq[:], in_=Vt[:], func=AF.Ln, bias=cst[:, 1:2]),
               reads=["Vt", "cst"], writes=["msq"])
        P.emit("act", lambda s, o=rt0[:]: s.activation(out=o, in_=msq[:], func=AF.Exp, scale=cst[:, 5:6]),
               reads=["msq", "cst"], writes=["Rt0"])
        for dc in range(8):
            P.emit("dve", lambda v, o=xb0[:, dc * C1:(dc + 1) * C1], m=mt0[:]:
                   v.tensor_sub(o, o, m),
                   reads=[f"xin0h{dc // 4}", "Mt0"],
                   writes=[f"x0d{dc}"] + (["xin0"] if dc == 7 else []))
        for dc in range(8):
            P.emit("dve", lambda v, o=ub0[:, dc * C1:(dc + 1) * C1],
                   a=xb0[:, dc * C1:(dc + 1) * C1], r=rt0[:]:
                   v.tensor_mul(o, a, r),
                   reads=[f"x0d{dc}", f"xin0h{dc // 4}", "Rt0"],
                   writes=[f"u0d{dc}"] + (["u0"] if dc == 7 else []))
        for c in range(NC1):
            if c + 2 < NC1:
                p1_dma_x(c + 2)
            p1_chunk(c)

        sc_v = scols.ap().rearrange("p (g c) -> p g c", c=18)
        P.emit("act", lambda s: s.copy(out=sc_v[0:64, :, 0:1], in_=ssb[0:64, :, None]),
               reads=["ssb", "scols"], writes=["scols"])
        P.emit("act", lambda s: s.copy(out=sc_v[64:128, :, 1:2], in_=ssb[64:128, :, None]),
               reads=["ssb", "scols"], writes=["scols"])

        for p in range(8):
            P.emit("act", lambda s, p=p: s.copy(out=kvb[0:64, p * 128:p * 128 + 64],
                                                in_=kvs[0:64, p * 128:p * 128 + 64]),
                   reads=["KVs", "KVb"], writes=["KVb"])
            P.emit("act", lambda s, p=p: s.copy(out=kvb[64:128, p * 128 + 64:(p + 1) * 128],
                                                in_=kvs[64:128, p * 128 + 64:(p + 1) * 128]),
                   reads=["KVs", "KVb"], writes=["KVb"])

        Vt2 = kvs[:, 0:512]
        ms2 = kvs[:, 512:1024]
        rR = kvs[0:16, 0:512]

        def denom(i, pk="PA0", split=False):
            t0 = i * 512
            rrb = rRb[i % 2]
            pw = PA[0:16, 0:512] if pk == "PA0" else PA[0:16, 512:1024]
            mm_group([(scols[:, 16 * p:16 * (p + 1)], qt[:, i * 4096 + p * 512: i * 4096 + (p + 1) * 512],
                       pw) for p in range(8)],
                     [pk], ["scols", f"qt{i}"])
            P.emit("dve", lambda v, t0=t0, pw=pw: v.tensor_add(rD[:], pw,
                                                               rE[:, t0:t0 + 512]),
                   reads=[pk, "rows_E"], writes=["r2D"])
            if split:
                for h in range(2):
                    P.emit("dve", lambda v, h=h: v.reciprocal(
                        out=rR[:, h * 256:(h + 1) * 256], in_=rD[:, h * 256:(h + 1) * 256]),
                        reads=["r2D"], writes=["r2R", "KVs"])
                    P.emit("act", lambda s, rrb=rrb, h=h: s.copy(
                        out=rrb[:, h * 256:(h + 1) * 256], in_=rR[:, h * 256:(h + 1) * 256]),
                        reads=["r2R"], writes=[f"r2Rb{i % 2}h{h}"])
            else:
                P.emit("dve", lambda v: v.reciprocal(out=rR, in_=rD[:]),
                       reads=["r2D"], writes=["r2R", "KVs"])
                P.emit("act", lambda s, rrb=rrb: s.copy(out=rrb[:], in_=rR),
                       reads=["r2R"], writes=[f"r2Rb{i % 2}h0", f"r2Rb{i % 2}h1"])

        es1.close()
        P.barrier(compute_only=True, skip=("pe",))

        denom(0, split=True)
        denom(1, pk="PA1", split=True)

        es2 = ExitStack()
        sb2 = lambda n, sh, dt: es2.enter_context(nc.sbuf_tensor(n, sh, dt))
        x2 = sb2("x2_sb", [128, 8 * T], F32)
        u2 = sb2("u2_sb", [128, 8 * T], BF16)
        wob = sb2("woutT_bf", [128, 8 * 1024], BF16)
        dma("pool", wob.ap().rearrange("p (dc o) -> p dc o", dc=8), woutT_r[:, :, :], "wout")

        es2a = ExitStack()
        sb2a = lambda n, sh, dt: es2a.enter_context(nc.sbuf_tensor(n, sh, dt))
        xin2 = [sb2a(f"xin2_{i}", [128, 8 * C2], F32) for i in range(2)]
        attn = [sb2a(f"attn{i}", [128, 8 * 512], BF16) for i in range(2)]
        xb2 = sb2a("xb2", [128, 8 * 512], BF16)
        Mt2b = sb2a("Mt2b", [128, 512], BF16)
        Rt2b = sb2a("Rt2b", [128, 512], BF16)
        Rsb = [sb2a(f"Rsb{i}", [128, 512], BF16) for i in range(2)]

        W1g = [qt[:, 0:4096], qt[:, 4096:8192]]
        W2g = [qt[:, 8192:12288], qt[:, 12288:16384]]

        XB2K = [f"xb2_{j}" for j in range(8)]

        def ln2_copy(i, j):
            P.emit("act", lambda g, i=i, j=j: g.copy(
                out=xb2[:, j * 512:(j + 1) * 512],
                in_=x2[:, j * T + i * 512: j * T + (i + 1) * 512]),
                reads=[f"x2p{i}_{j}"], writes=[f"xb2_{j}"])

        def ln2_rest(i):
            u2b_ap = u2.ap().rearrange("p (i j t) -> p i j t", i=4, j=8)[:, i]
            uk = "u2a" if i < 2 else "u2b"
            mm_group([(ones[:], xb2[:, j * 512:(j + 1) * 512], PD[:, 0:512]) for j in range(8)],
                     ["PD0"], XB2K + ["ones"])
            P.emit("dve", lambda g, o=u2b_ap: g.tensor_mul(
                o, xb2.ap().rearrange("p (j t) -> p j t", j=8),
                xb2.ap().rearrange("p (j t) -> p j t", j=8)),
                reads=XB2K, writes=[uk])
            mm_group([(ones[:], u2[:, i * 4096 + j * 512: i * 4096 + (j + 1) * 512],
                       PD[:, 512:1024]) for j in range(8)],
                     ["PD1"], [uk, "ones"])
            P.emit("act", lambda s: s.mul(Mt2b[:], PD[:, 0:512], cst[:, 0:1]),
                   reads=["PD0", "cst"], writes=["Mt2b"])
            P.emit("dve", lambda v: v.tensor_mul(ms2, Mt2b[:], Mt2b[:]), reads=["Mt2b"], writes=["ms2"])
            P.emit("dve", lambda v: v.scalar_tensor_tensor(
                out=Vt2, in0=PD[:, 512:1024], scalar=cst[:, 0:1], in1=ms2,
                op0=OP.mult, op1=OP.subtract),
                reads=["PD1", "ms2", "cst", "r2R", "r2Rb0", "r2Rb1"], writes=["Vt2"])
            P.emit("act", lambda s: s.activation(out=ms2, in_=Vt2, func=AF.Ln, bias=cst[:, 1:2]),
                   reads=["Vt2", "cst"], writes=["ms2"])
            P.emit("act", lambda s: s.activation(out=Rt2b[:], in_=ms2, func=AF.Exp, scale=cst[:, 5:6]),
                   reads=["ms2", "cst"], writes=["Rt2"])
            P.emit("dve", lambda v: v.tensor_sub(
                xb2.ap().rearrange("p (j t) -> p j t", j=8),
                xb2.ap().rearrange("p (j t) -> p j t", j=8),
                Mt2b[:, None, :].broadcast_to([128, 8, 512])),
                reads=["Mt2b"] + XB2K, writes=XB2K)
            P.emit("dve", lambda v, o=u2b_ap: v.tensor_mul(
                o, xb2.ap().rearrange("p (j t) -> p j t", j=8),
                Rt2b[:, None, :].broadcast_to([128, 8, 512])),
                reads=["Rt2"] + XB2K, writes=[uk])

        def wout_j(i, j):
            ab = i % 2
            pk = "PC0" if j % 2 == 0 else "PC1"
            po = PC[:, 0:512] if j % 2 == 0 else PC[:, 512:1024]
            if j % 2 == 1 and j % 4 == 3:
                pass
            pt = PC if j % 2 == 0 else PD
            pks = ["PC0", "PC1"] if j % 2 == 0 else ["PD0", "PD1"]
            bank = (j % 4)
            src = [PC[:, 0:512], PC[:, 512:1024], PD[:, 0:512], PD[:, 512:1024]][bank]
            key = ["PC0", "PC1", "PD0", "PD1"][bank]
            mm_group([(wob[:, dc * 1024 + j * 128: dc * 1024 + (j + 1) * 128],
                       attn[ab][:, dc * 512:(dc + 1) * 512], src) for dc in range(8)],
                     [key], [f"at{ab}_{dc}" for dc in range(8)] + ["wout"])
            for cc in range(2):
                cg = 2 * i + cc
                P.emit("dve", lambda v, j=j, cg=cg, cc=cc, i_=src, xi=xin2[cg % 2]:
                       v.scalar_tensor_tensor(
                           out=x2[:, j * T + cg * C2: j * T + (cg + 1) * C2],
                           in0=i_[:, cc * C2:(cc + 1) * C2],
                           scalar=bout[:, j:j + 1], in1=xi[:, j * C2:(j + 1) * C2],
                           op0=OP.add, op1=OP.add),
                       reads=[key, "bout", f"xin2_{cg % 2}"], writes=[f"x2p{i}_{j}"])
            ln2_copy(i, j)

        dma("sp", xin2[0].ap().rearrange("p (j t) -> p j t", j=8), xT_r[:, :, 0:C2], "xin2_0")
        dma("sp", xin2[1].ap().rearrange("p (j t) -> p j t", j=8), xT_r[:, :, C2:2 * C2], "xin2_1")

        for i in range(NC2 // 2):
            t0 = i * 512
            ab = i % 2
            A = attn[ab]
            rrb = rRb[i % 2]
            for p in range(8):
                pkR = "PA0" if p % 2 == 0 else "PA1"
                poR = PA[:, 0:512] if p % 2 == 0 else PA[:, 512:1024]
                if i < 2:
                    for h in range(2):
                        mm_group([(expfb[:, p * 128:(p + 1) * 128],
                                   rrb[:, h * 256:(h + 1) * 256],
                                   poR[:, h * 256:(h + 1) * 256])],
                                 [pkR], [f"r2Rb{i % 2}h{h}", "expf"])
                else:
                    mm_group([(expfb[:, p * 128:(p + 1) * 128], rrb[:], poR)],
                             [pkR], [f"r2Rb{i % 2}h0", f"r2Rb{i % 2}h1", "expf"])
                rb = Rsb[p % 2]
                P.emit("act", lambda s, o=rb[:], i_=poR: s.copy(out=o, in_=i_),
                       reads=[pkR], writes=[f"Rsb{p % 2}"])
                pkN = "PB0" if p % 2 == 0 else "PB1"
                po = PB[:, 0:512] if p % 2 == 0 else PB[:, 512:1024]
                mm_group([(kvb[:, p * 128:(p + 1) * 128],
                           qt[:, i * 4096 + p * 512: i * 4096 + (p + 1) * 512], po)],
                         [pkN], ["KVb", f"qt{i}"])
                P.emit("dve", lambda v, o=A[:, p * 512:(p + 1) * 512], i_=po, r=rb[:]:
                       v.tensor_mul(o, i_, r),
                       reads=[pkN, f"Rsb{p % 2}"], writes=[f"at{ab}_{p}"])

            if i + 2 < NC2 // 2:
                denom(i + 2)

            if i > 0:
                for j in range(8):
                    wout_j(i - 1, j)
                ln2_rest(i - 1)

            if i == 0:
                P.emit("pool", lambda e: e.dma_start(out=W1g[0].rearrange("p (dc i) -> p dc i", dc=8),
                                                     in_=w1_r[:, :, 0:GI]),
                       writes=["W1g0", "qt0"], dma=True)
            elif i == 1:
                P.emit("pool", lambda e: e.dma_start(out=W1g[1].rearrange("p (dc i) -> p dc i", dc=8),
                                                     in_=w1_r[:, :, GI:2 * GI]),
                       writes=["W1g1", "qt1"], dma=True)
            elif i == 2:
                P.emit("pool", lambda e: e.dma_start(out=W2g[0].rearrange("p (ic o) -> p ic o", ic=4),
                                                     in_=w2_r[:, 0:4, :]),
                       writes=["W2g0", "qt2"], dma=True)
            else:
                P.emit("pool", lambda e: e.dma_start(out=W2g[1].rearrange("p (ic o) -> p ic o", ic=4),
                                                     in_=w2_r[:, 4:8, :]),
                       writes=["W2g1", "qt3"], dma=True)

            if i >= 1:
                dma("sp", xin2[0].ap().rearrange("p (j t) -> p j t", j=8),
                    xT_r[:, :, (2 * i) * C2:(2 * i + 1) * C2], "xin2_0")
                dma("sp", xin2[1].ap().rearrange("p (j t) -> p j t", j=8),
                    xT_r[:, :, (2 * i + 1) * C2:(2 * i + 2) * C2], "xin2_1")

        for j in range(8):
            wout_j(NC2 // 2 - 1, j)

        es2a.close()

        h_sb = wob

        es2b = ExitStack()
        sb2b = lambda n, sh, dt: es2b.enter_context(nc.sbuf_tensor(n, sh, dt))
        ostg = [sb2b(f"ostg{i}", [128, TH], F32) for i in range(3)]

        def load_w1(g):
            dma("pool", W1g[g % 2].rearrange("p (dc i) -> p dc i", dc=8),
                w1_r[:, :, g * GI:(g + 1) * GI], f"W1g{g % 2}")

        def load_w2(g):
            dma("pool", W2g[g % 2].rearrange("p (ic o) -> p ic o", ic=4),
                w2_r[:, g * 4:(g + 1) * 4, :], f"W2g{g % 2}")

        ns = 0
        for gp in range(NG // 2):
            for th in range(NTH):
                for gg in range(2):
                    g = 2 * gp + gg
                    buf = g % 2
                    for ic in range(4):
                        prs, sts, sps = [], [], []
                        for dc in range(8):
                            for r in range(2):
                                pr = 2 * th + r
                                prs.append((W1g[buf][:, dc * 512 + ic * 128: dc * 512 + (ic + 1) * 128],
                                            u2[:, pr * 4096 + dc * 512: pr * 4096 + (dc + 1) * 512],
                                            PA[:, r * 512:(r + 1) * 512] if ic % 2 == 0
                                            else PB[:, r * 512:(r + 1) * 512]))
                                sts.append(dc == 0)
                                sps.append(dc == 7)
                        pk = ["PA0", "PA1"] if ic % 2 == 0 else ["PB0", "PB1"]
                        mm_group(prs, pk, ["u2a" if th == 0 else "u2b", f"W1g{buf}"],
                                 starts=sts, stops=sps)
                        src = PA if ic % 2 == 0 else PB
                        P.emit("act", lambda s, o=h_sb[:, gg * 4096 + ic * 1024: gg * 4096 + (ic + 1) * 1024],
                               i=src[:], b=b1s[:, g * 4 + ic:g * 4 + ic + 1]:
                               s.activation(out=o, in_=i, func=AF.Relu, bias=b),
                               reads=pk + ["b1"], writes=[f"h{gg}_{ic}", "wout"])
                    if gp == 0 and th == 0 and gg == 0:
                        ln2_rest(NC2 // 2 - 1)
                    if th == NTH - 1 and g + 2 < NG:
                        load_w1(g + 2)
                for j in range(8):
                    pt, pks = (PC, ["PC0", "PC1"]) if j % 2 == 0 else (PD, ["PD0", "PD1"])
                    prs, sts, sps = [], [], []
                    for icg in range(8):
                        gg2, ic = icg // 4, icg % 4
                        bufw = (2 * gp + gg2) % 2
                        for r in range(2):
                            prs.append((W2g[bufw][:, ic * 1024 + j * 128: ic * 1024 + (j + 1) * 128],
                                        h_sb[:, gg2 * 4096 + ic * 1024 + r * 512:
                                             gg2 * 4096 + ic * 1024 + (r + 1) * 512],
                                        pt[:, r * 512:(r + 1) * 512]))
                            sts.append(icg == 0)
                            sps.append(icg == 7)
                    hkeys = [f"h{gg2}_{ic}" for gg2 in range(2) for ic in range(4)]
                    if j == 0:
                        mm_group(prs[:14], pks, hkeys[:7] + ["W2g0", "W2g1"],
                                 starts=sts[:14], stops=[False] * 14)
                        mm_group(prs[14:], pks, [hkeys[7]],
                                 starts=[False, False], stops=sps[14:])
                    else:
                        mm_group(prs, pks, hkeys + ["W2g0", "W2g1"],
                                 starts=sts, stops=sps)
                    x2k = [f"x2p{2 * th}_{j}", f"x2p{2 * th + 1}_{j}"]
                    if gp < NG // 2 - 1:
                        P.emit("dve", lambda v, j=j, th=th, i=pt[:]: v.tensor_add(
                            x2[:, j * T + th * TH: j * T + (th + 1) * TH],
                            x2[:, j * T + th * TH: j * T + (th + 1) * TH], i),
                            reads=pks + x2k, writes=x2k)
                    else:
                        ob = ostg[ns % 3]
                        P.emit("dve", lambda v, j=j, th=th, i=pt[:], o=ob[:]: v.scalar_tensor_tensor(
                            out=o, in0=i, scalar=b2s[:, j:j + 1],
                            in1=x2[:, j * T + th * TH: j * T + (th + 1) * TH],
                            op0=OP.add, op1=OP.add),
                            reads=pks + x2k + ["b2"], writes=[f"ostg{ns % 3}"])
                        dma("sp", outT_r[:, j, th * TH:(th + 1) * TH], ob[:],
                            f"od{ns}", rkeys=[f"ostg{ns % 3}"])
                        ns += 1
                if th == NTH - 1 and gp < NG // 2 - 1:
                    load_w2(2 * gp + 2)
                    load_w2(2 * gp + 3)

        es2b.close()
        es2.close()
        P.final_wait("sp", "sp")

        with nc.Block() as block:
            def runner(name):
                def run(eng):
                    for waits, fn, sem in P.ops[name]:
                        for (s, v) in waits:
                            eng.wait_ge(semh[s], v)
                        if fn is not None:
                            ins = fn(eng)
                            ins.then_inc(semh[sem], 1 if sem.startswith("c_") else 16)
                return run
            block.tensor(runner("pe"))
            block.scalar(runner("act"))
            block.vector(runner("dve"))
            block.gpsimd(runner("pool"))
            block.sync(runner("sp"))

    return nc


def kernel(L0_attn_x, L0_attn_w_qkv, L0_attn_w_out, L0_attn_b_out,
           L0_attn_random_features, L0_ffn_W1, L0_ffn_b1, L0_ffn_W2, L0_ffn_b2):
    x = np.asarray(L0_attn_x, dtype=np.float32)
    wqkvT = np.ascontiguousarray(np.asarray(L0_attn_w_qkv, np.float32).T)
    woutT = np.ascontiguousarray(np.asarray(L0_attn_w_out, np.float32).T)
    w1 = np.ascontiguousarray(np.asarray(L0_ffn_W1, np.float32))
    w2 = np.ascontiguousarray(np.asarray(L0_ffn_W2, np.float32))
    rf = np.asarray(L0_attn_random_features, np.float32)
    b_out = np.asarray(L0_attn_b_out, np.float32)
    b1 = np.asarray(L0_ffn_b1, np.float32)
    b2 = np.asarray(L0_ffn_b2, np.float32)

    rf2 = np.zeros((128, 128), np.float32)
    rf2[0:64, 0:64] = rf.T
    rf2[64:128, 64:128] = rf.T
    obk = np.zeros((128, 128), np.float32)
    for p in range(8):
        obk[0:64, 16 * p + 2 * p] = 1.0
        obk[64:128, 16 * p + 2 * p + 1] = 1.0
    expf = np.zeros((16, 1024), np.float32)
    for p in range(8):
        expf[2 * p, p * 128:p * 128 + 64] = 1.0
        expf[2 * p + 1, p * 128 + 64:(p + 1) * 128] = 1.0
    bout_h = np.ascontiguousarray(b_out.reshape(8, 128).T)
    b1_h = np.ascontiguousarray(b1.reshape(32, 128).T)
    b2_h = np.ascontiguousarray(b2.reshape(8, 128).T)

    nc = build_nc()
    in_maps = []
    for c in range(NCORES):
        in_maps.append({
            "xT": np.ascontiguousarray(x[c].T),
            "wqkvT": wqkvT, "woutT": woutT, "w1": w1, "w2": w2,
            "rf2": rf2, "obk": obk, "expf": expf,
            "bout": bout_h, "b1": b1_h, "b2": b2_h,
        })
    res = run_bass_kernel_spmd(nc, in_maps, core_ids=list(range(NCORES)))
    out = np.stack([np.ascontiguousarray(r["outT"].T) for r in res.results], axis=0)
    return out
